# version 02
# baseline (speedup 1.0000x reference)
import os
import numpy as np
import concourse.bass as bass
import concourse.mybir as mybir
from concourse.bass_utils import run_bass_kernel_spmd

F32 = mybir.dt.float32
BF16 = mybir.dt.bfloat16
U32 = mybir.dt.uint32
AF = mybir.ActivationFunctionType
ALU = mybir.AluOpType
AX = mybir.AxisListType

D = 1024
NE = 8
HID = 2048
FFH = 4096
NH = 4
DH = 256
T_OWN = 2048
T_EXT = 4096
TS = 1024
EPS = 1e-5
KC = 8
SCALE = 0.25

ENGINES = ("tensor", "scalar", "vector", "sync", "gpsimd")


class B:
    def __init__(self, nc):
        self.nc = nc
        self.ev = {e: [] for e in ENGINES}
        self.semh = {}
        self.cnt = {}
        for name in ("SW", "SX", "SP", "SA", "SV", "SG", "PSEM"):
            self.semh[name] = nc.alloc_semaphore(name)
            self.cnt[name] = 0

    OWN = {"vector": "SV", "scalar": "SA"}

    def emit(self, eng, fn, waits=(), inc=None, inck=1):
        own = self.OWN.get(eng)
        pre = self.cnt[own] if own else 0
        if own and inc is None:
            inc = own
        tok = None
        if inc is not None:
            self.cnt[inc] += inck
            tok = self.cnt[inc]
        for w in waits:
            if w is None:
                continue
            s, v = w
            assert v <= self.cnt[s], f"wait on future {s}:{v} > {self.cnt[s]}"
        ws = tuple(w for w in waits if w)
        if own and pre > 0:
            ws = ((own, pre),) + ws
        self.ev[eng].append((ws, fn, inc, inck))
        return tok

    def replay(self, eng_name, eng):
        sems = self.semh
        hwm = {}
        for (waits, fn, inc, inck) in self.ev[eng_name]:
            for (s, v) in waits:
                if hwm.get(s, -1) < v:
                    eng.wait_ge(sems[s], v)
                    hwm[s] = v
            inst = fn(eng)
            if inc is not None:
                if inst is None:
                    inst = eng.nop()
                inst.then_inc(sems[inc], inck)

    def tok(self, s):
        return self.cnt[s]



def _c(x):
    return np.ascontiguousarray(x)


def prep_shared(inp):
    d = {}
    W1 = np.asarray(inp["L0_ffn_W1"])
    W2 = np.asarray(inp["L0_ffn_W2"])
    d["W1s"] = _c(W1.reshape(NE, 8, 128, 16, 128).transpose(0, 3, 2, 1, 4))
    d["W2s"] = _c(W2.reshape(NE, 16, 128, 8, 128).transpose(0, 3, 2, 1, 4))
    d["Wr"] = _c(np.asarray(inp["L0_ffn_Wr"]).reshape(8, 128, 8))
    d["br"] = _c(np.asarray(inp["L0_ffn_br"]).reshape(1, 8))
    d["b1t"] = _c(np.asarray(inp["L0_ffn_b1"]).reshape(NE, 16, 128).transpose(2, 0, 1).reshape(128, 128))
    d["b2"] = _c(np.asarray(inp["L0_ffn_b2"]))
    d["wk"] = _c(np.asarray(inp["L0_attn_wk"]).reshape(8, 128, D).transpose(1, 0, 2))
    d["wv"] = _c(np.asarray(inp["L0_attn_wv"]).reshape(8, 128, D).transpose(1, 0, 2))
    d["wq"] = _c(np.asarray(inp["L0_attn_wq"]).reshape(8, 128, 8, 128).transpose(2, 1, 0, 3))
    d["wo"] = _c(np.asarray(inp["L0_attn_wo"]).reshape(8, 128, 8, 128).transpose(2, 1, 0, 3))
    d["ff1"] = _c(np.asarray(inp["L0_attn_w_ff1"]).reshape(8, 128, 32, 128).transpose(2, 1, 0, 3))
    d["ff2"] = _c(np.asarray(inp["L0_attn_w_ff2"]).reshape(32, 128, 8, 128).transpose(2, 1, 0, 3))
    d["wd"] = _c(np.asarray(inp["L0_attn_w_d"]).reshape(8, 128, 8, 128).transpose(2, 1, 0, 3))
    d["bo"] = _c(np.asarray(inp["L0_attn_bo"]).reshape(8, 128).T)
    d["bff1"] = _c(np.asarray(inp["L0_attn_b_ff1"]).reshape(32, 128).T)
    d["bff2"] = _c(np.asarray(inp["L0_attn_b_ff2"]).reshape(8, 128).T)
    d["bd"] = _c(np.asarray(inp["L0_attn_b_d"]).reshape(8, 128).T)
    for nm, key in (("g1", "L0_attn_ln1_g"), ("c1", "L0_attn_ln1_b"),
                    ("g2", "L0_attn_ln2_g"), ("c2", "L0_attn_ln2_b")):
        d[nm] = _c(np.asarray(inp[key]).reshape(8, 128).T)
    return d


def core_meta(c):
    b = 0 if c in (0, 1, 4, 5) else 1
    island_base = 0 if c < 4 else 4096
    own0 = island_base + (c & 1) * 2048
    red0 = (4096 - island_base) + (c & 1) * 2048
    return b, own0, red0


def prep_core(inp, c):
    x = np.asarray(inp["L0_attn_x"])
    b, own0, red0 = core_meta(c)
    return {
        "x_own": _c(x[b, own0:own0 + 2048, :].T),
        "x_red": _c(x[b, red0:red0 + 2048, :].T),
    }


AF_XS = 0
AF_BA = 8192
AF_BC = 9216
AF_TMP = 10240
AF_TMP2 = 10752
AF_N = 11072

ABF_XLNB = 0
ABF_SQB = 8192
ABF_HTIL = 10240
ABF_WKV = 10240
ABF_KTM = 26624
ABF_VAUG = 34816
ABF_QB = 10240
ABF_H1S = 26624
ABF_WQO = 28672
ABF_GB = 10240
ABF_XB = 10240
ABF_HRAW = 43104
ABF_BM = 43104 + 2048
ABF_N = 43104 + 2048 + 2048

WR_W1 = [0, 1024]
WR_W2 = [2048, 4096, 6144, 8192]
WR_N = 10240


def build_program(stage):
    nc = bass.Bass("TRN2", target_bir_lowering=False, num_devices=8)
    P = {}

    def din(name, shape, dt=F32):
        P[name] = nc.dram_tensor(name, list(shape), dt, kind="ExternalInput").ap()

    din("x_own", (D, T_OWN)); din("x_red", (D, T_OWN))
    din("W1s", (NE, 16, 128, 8, 128)); din("W2s", (NE, 8, 128, 16, 128))
    din("Wr", (8, 128, 8)); din("br", (1, 8))
    din("b1t", (128, 128)); din("b2", (8, D))
    din("wk", (128, 8, D)); din("wv", (128, 8, D))
    din("wq", (8, 128, 8, 128)); din("wo", (8, 128, 8, 128))
    din("ff1", (32, 128, 8, 128)); din("ff2", (8, 128, 32, 128))
    din("wd", (8, 128, 8, 128))
    din("bo", (128, 8)); din("bff1", (128, 32)); din("bff2", (128, 8)); din("bd", (128, 8))
    for nm in ("g1", "c1", "g2", "c2"):
        din(nm, (128, 8))
    din("nonce", (1, 2), U32)

    P["y"] = nc.dram_tensor("y", [D, T_OWN], F32, kind="ExternalOutput").ap()
    if stage in ("moe1", "p2"):
        P["dbg"] = nc.dram_tensor("dbg", [D, T_EXT], F32, kind="ExternalOutput").ap()

    P["R1d"] = nc.dram_tensor("R1d", [KC, 128, T_EXT], F32, kind="Internal").ap()
    P["h0d"] = nc.dram_tensor("h0d", [KC, 128, T_OWN], F32, kind="Internal").ap()
    P["h1d"] = nc.dram_tensor("h1d", [KC, 128, T_OWN], BF16, kind="Internal").ap()
    P["x2d"] = nc.dram_tensor("x2d", [KC, 128, T_OWN], F32, kind="Internal").ap()
    P["x3d"] = nc.dram_tensor("x3d", [KC, 128, T_OWN], F32, kind="Internal").ap()
    P["R3d"] = nc.dram_tensor("R3d", [KC, 128, T_OWN], F32, kind="Internal").ap()
    P["mseld"] = nc.dram_tensor("mseld", [64, 128], F32, kind="Internal").ap()
    P["shu"] = nc.dram_tensor("shu", [2, 128, 2056], F32, kind="Internal", addr_space="Shared").ap()
    P["shf"] = nc.dram_tensor("shf", [2, 1, 2], U32, kind="Internal", addr_space="Shared").ap()

    sb = {}

    def S(name, shape, dt):
        sb[name] = nc.alloc_sbuf_tensor(name, list(shape), dt)

    S("ones_row", (1, 512), BF16)
    S("ones_col", (128, 1), BF16)
    S("b1_sb", (128, 128), F32)
    S("b2_sb", (8, D), BF16)
    S("br_sb", (1, 8), BF16)
    S("Wr_sb", (128, 64), BF16)
    S("bo_sb", (128, 8), F32); S("bff1_sb", (128, 32), F32)
    S("bff2h_sb", (128, 8), F32); S("bd_sb", (128, 8), F32)
    for nm in ("g1", "c1", "g2", "c2"):
        S(nm + "_sb", (128, 8), F32)
    S("nonce_sb", (1, 2), U32)
    S("flag_sb", (1, 2), U32)
    S("epsrow", (1, 1), F32)
    S("stat", (1, TS), F32); S("stat2", (1, TS), F32)
    S("arow", (1, TS), F32); S("crow", (1, TS), F32)
    S("arow_b", (1, TS), BF16); S("crow_b", (1, TS), BF16)
    S("M8_b", (8, TS), BF16)
    S("Mrow_b", (1, NE * TS), BF16)
    S("Ltm", (128, 64), F32); S("Rt1", (128, 64), F32)
    S("Rt2", (128, 64), F32); S("Rt3", (128, 64), F32)
    S("Mtm", (128, 8), F32); S("Stm", (128, 8), F32); S("MPtm", (128, 8), F32)
    S("Srec", (128, 8), F32)
    S("Msl_t", (64, 128), F32)
    S("ident", (128, 128), F32)
    S("U_sb", (128, 2056), F32)
    S("AFa", (128, AF_N), F32)
    S("ABFa", (128, ABF_N), BF16)
    S("WR", (128, WR_N), BF16)

    ps = {}
    for nm in ("PH0", "PH1", "PY0", "PY1"):
        ps[nm] = nc.alloc_psum_tensor(nm, [128, 1024], F32)
    bld = B(nc)
    return nc, P, sb, ps, bld


class Kern:
    def __init__(self, nc, P, sb, ps, bld):
        self.nc, self.P, self.sb, self.ps, self.b = nc, P, sb, ps, bld
        self.pfree = {nm: None for nm in ("PH0", "PH1", "PY0", "PY1")}
        self.wslot_free = {}
        self._sq_free = [None, None]
        self._castb_free = None
        self._stat_free = None
        self._rt_evac = [None, None]
        self._xlnb_tok = None
        self._a_mask = [None, None]
        self._cur_pair = 0
        self._b_done = None
        self._xs_free = None
        self._r3_tok = None
        self._dbg_toks = []
        self._dbg_slice0 = False
        self._bq_free = {0: None, 1: None}
        self._ar_free = {}
        self._hr_free = {}
        self._xb_free = {}
        self._tok_c = None
        self._ta_free = None
        self._cb = None
        self._x2_tok = None
        self._x3_tok = None

    def e(self, eng, fn, waits=(), inc=None, inck=1):
        return self.b.emit(eng, fn, waits, inc, inck)

    def pwait(self, nm):
        return self.pfree[nm]

    def pdone(self, nm, tok):
        self.pfree[nm] = tok

    def dbg(self, name, ap, wait, dt=None):
        import os as _os
        taps = _os.environ.get("MK_DBG", "").split(",")
        if name not in taps:
            return
        shape = list(ap.shape)
        t = self.nc.dram_tensor("dbg_" + name, shape, dt or ap.dtype, kind="ExternalOutput").ap()
        tok = self.e("sync", lambda s, d=t, a=ap: s.dma_start(out=d, in_=a),
                     waits=[wait], inc="SX", inck=16)
        self._dbg_toks.append(("SX", tok))

    def af(self, c0, w):
        return self.sb["AFa"][:, c0:c0 + w]

    def ab(self, c0, w):
        return self.sb["ABFa"][:, c0:c0 + w]



def emit_ln(K, xin, T, out_b, out_f=None, gb=None, xin_tok=None, outb_off=ABF_XLNB):
    sb, ps = K.sb, K.ps
    ntt = T // 512
    castb = [K.ab(outb_off + kc * T, T) for kc in range(KC)]
    tok_cast = None
    for kc in range(KC):
        w = []
        if kc == 0:
            if xin_tok:
                w.append(xin_tok)
            if K._castb_free:
                w.extend(K._castb_free if isinstance(K._castb_free, list) else [K._castb_free])
        tok_cast = K.e("scalar", lambda a, s=xin[kc], d=castb[kc]: a.copy(d, s),
                       waits=w, inc="SA")
    tok_sumg = tok_sqg = None
    for tt in range(ntt):
        bank = ps["PH0"][0:1, tt * 512:(tt + 1) * 512]
        for kc in range(KC):
            w0 = []
            if kc == 0:
                w0 = [("SA", tok_cast)]
                if tt == 0:
                    w0.append(K.pwait("PH0"))
            tok_sumg = K.e("tensor", lambda t, bk=bank, l=sb["ones_col"][:, :],
                           r=castb[kc][:, tt * 512:(tt + 1) * 512],
                           st=(kc == 0), sp=(kc == KC - 1):
                           t.matmul(bk, l, r, start=st, stop=sp),
                           waits=w0, inc="SP" if kc == KC - 1 else None)
    for tt in range(ntt):
        bank = ps["PH1"][0:1, tt * 512:(tt + 1) * 512]
        for kc in range(KC):
            r = kc % 2
            sq = K.ab(ABF_SQB + r * 1024, T)
            wsq = [K._sq_free[r]] if K._sq_free[r] else []
            tok_sq = K.e("scalar", lambda a, s=xin[kc], d=sq, t0=tt:
                         a.activation(d[:, t0 * 512:(t0 + 1) * 512],
                                      s[:, t0 * 512:(t0 + 1) * 512], AF.Square),
                         waits=wsq, inc="SA")
            w0 = [("SA", tok_sq)]
            if kc == 0 and tt == 0:
                w0.append(K.pwait("PH1"))
            tok_sqg = K.e("tensor", lambda t, bk=bank, l=sb["ones_col"][:, :],
                          r=sq[:, tt * 512:(tt + 1) * 512],
                          st=(kc == 0), sp=(kc == KC - 1):
                          t.matmul(bk, l, r, start=st, stop=sp),
                          waits=w0, inc="SP")
            K._sq_free[r] = ("SP", tok_sqg)
    wst = [("SP", tok_sumg)]
    if K._stat_free:
        wst.append(K._stat_free)
    tok_sum = K.e("vector", lambda v: v.tensor_copy(sb["stat"][:, 0:T], ps["PH0"][0:1, 0:T]),
                  waits=wst, inc="SV")
    K.pdone("PH0", ("SV", tok_sum))
    tok_sq2 = K.e("vector", lambda v: v.tensor_copy(sb["stat2"][:, 0:T], ps["PH1"][0:1, 0:T]),
                  waits=[("SP", tok_sqg)], inc="SV")
    K.pdone("PH1", ("SV", tok_sq2))
    st_, st2, ar, cr = sb["stat"], sb["stat2"], sb["arow"], sb["crow"]
    K.e("vector", lambda v: v.tensor_scalar(st_[:, 0:T], st_[:, 0:T], 1.0 / D, None, ALU.mult))
    K.e("vector", lambda v: v.tensor_scalar(st2[:, 0:T], st2[:, 0:T], 1.0 / D, None, ALU.mult))
    K.e("vector", lambda v: v.tensor_tensor(cr[:, 0:T], st_[:, 0:T], st_[:, 0:T], ALU.mult))
    tok_v = K.e("vector", lambda v: v.tensor_tensor(st2[:, 0:T], st2[:, 0:T], cr[:, 0:T], ALU.subtract),
                inc="SV")
    tok_ln = K.e("scalar", lambda a: a.activation(st2[:, 0:T], st2[:, 0:T], AF.Ln,
                                                  bias=sb["epsrow"][0:1, 0:1]),
                 waits=[("SV", tok_v)], inc="SA")
    tok_rs = K.e("scalar", lambda a: a.activation(ar[:, 0:T], st2[:, 0:T], AF.Exp, scale=-0.5),
                 waits=[("SA", tok_ln)], inc="SA")
    K.e("vector", lambda v: v.tensor_scalar(st_[:, 0:T], st_[:, 0:T], -1.0, None, ALU.mult),
        waits=[("SA", tok_rs)])
    tok_c = K.e("vector", lambda v: v.tensor_tensor(cr[:, 0:T], st_[:, 0:T], ar[:, 0:T], ALU.mult),
                inc="SV")
    K.e("scalar", lambda a: a.copy(sb["arow_b"][:, 0:T], ar[:, 0:T]), waits=[("SV", tok_c)])
    tok_cb = K.e("scalar", lambda a: a.copy(sb["crow_b"][:, 0:T], cr[:, 0:T]), inc="SA")
    ba = K.af(AF_BA, T)
    bc = K.af(AF_BC, T)
    tok_bab = None
    for pnm, row, dst in (("PY0", sb["arow_b"], ba), ("PY1", sb["crow_b"], bc)):
        unit = ps[pnm]
        mt = None
        for tt in range(ntt):
            w0 = []
            if tt == 0:
                w0 = [("SA", tok_cb), K.pwait(pnm)]
            mt = K.e("tensor", lambda t, bk=unit[:, tt * 512:(tt + 1) * 512],
                     l=sb["ones_row"][0:1, 0:128], r=row[0:1, tt * 512:(tt + 1) * 512]:
                     t.matmul(bk, l, r, start=True, stop=True),
                     waits=w0, inc="SP")
        tok_bab = K.e("vector", lambda v, d=dst, u=unit: v.tensor_copy(d, u[:, 0:T]),
                      waits=[("SP", mt)], inc="SV")
        K.pdone(pnm, ("SV", tok_bab))
    tmp = K.af(AF_TMP, 512)
    tok_out = None
    prev_act = None
    for kc in range(KC):
        for tt in range(ntt):
            sl = slice(tt * 512, (tt + 1) * 512)
            w1 = []
            if kc == 0 and tt == 0:
                w1.append(("SV", tok_bab))
            if prev_act is not None:
                w1.append(("SA", prev_act))
                prev_act = None
            K.e("vector", lambda v, a=xin[kc], s=sl:
                v.tensor_tensor(tmp, a[:, s], ba[:, s], ALU.mult), waits=w1)
            if gb is None:
                if out_f is not None:
                    t2 = K.e("vector", lambda v, d=out_f[kc], s=sl:
                             v.tensor_tensor(d[:, s], tmp, bc[:, s], ALU.add), inc="SV")
                    tok_out = ("SV", t2)
                    if out_b is not None:
                        t3 = K.e("scalar", lambda a, d=out_b[kc], sf=out_f[kc], s=sl:
                                 a.copy(d[:, s], sf[:, s]), waits=[("SV", t2)], inc="SA")
                        tok_out = ("SA", t3)
                else:
                    t2 = K.e("vector", lambda v, d=out_b[kc], s=sl:
                             v.tensor_tensor(d[:, s], tmp, bc[:, s], ALU.add), inc="SV")
                    tok_out = ("SV", t2)
            else:
                g_ap, b_ap = gb
                t2 = K.e("vector", lambda v, s=sl:
                         v.tensor_tensor(tmp, tmp, bc[:, s], ALU.add), inc="SV")
                t3 = K.e("scalar", lambda a, d=out_b[kc], s=sl, kk=kc:
                         a.activation(d[:, s], tmp, AF.Identity,
                                      scale=g_ap[:, kk:kk + 1], bias=b_ap[:, kk:kk + 1]),
                         waits=[("SV", t2)], inc="SA")
                prev_act = t3
                tok_out = ("SA", t3)
    K._castb_free = ("SP", tok_sumg)
    K._stat_free = ("SA", tok_cb)
    return tok_out



def emit_router(K, xlnb, T):
    sb, ps, P = K.sb, K.ps, K.P
    ntc = T // 128
    unit = ps["PY0"]
    tok_l = None
    for tc in range(ntc):
        half = tc % 2
        view = unit[:, half * 512: half * 512 + 8]
        w0 = []
        if tc == 0:
            w0 = [K._xlnb_tok]
        if tc < 2:
            w0.append(K.pwait("PY0"))
        else:
            w0.append(("SV", K._rt_evac[half]))
        K.e("tensor", lambda t, bk=view: t.matmul(
            bk, sb["ones_row"][0:1, 0:128], sb["br_sb"][0:1, :], start=True, stop=False),
            waits=w0)
        mt = None
        for kc in range(KC):
            mt = K.e("tensor", lambda t, bk=view, l=xlnb[kc][:, tc * 128:(tc + 1) * 128],
                     r=sb["Wr_sb"][:, kc * 8:(kc + 1) * 8], sp=(kc == KC - 1):
                     t.matmul(bk, l, r, start=False, stop=sp),
                     inc="SP" if kc == KC - 1 else None)
        tok_l = K.e("vector", lambda v, d=sb["Ltm"][:, tc * 8:(tc + 1) * 8], u=view:
                    v.tensor_copy(d, u), waits=[("SP", mt)], inc="SV")
        K._rt_evac[half] = tok_l
    K.pdone("PY0", ("SV", tok_l))
    Ltm, Rt1, Rt2, Rt3 = sb["Ltm"], sb["Rt1"], sb["Rt2"], sb["Rt3"]
    Mtm, Stm, MPtm = sb["Mtm"], sb["Stm"], sb["MPtm"]
    tok_mx = K.e("vector", lambda v: v.reduce_max(
        Mtm[:, 0:8], Ltm[:, 0:64].rearrange("p (a e) -> p a e", a=8), axis=AX.X), inc="SV")
    tok_e = K.e("scalar", lambda a: a.activation(Rt1[:, 0:64], Ltm[:, 0:64], AF.Exp),
                waits=[("SV", tok_l)], inc="SA")
    K.e("vector", lambda v: v.reduce_sum(
        Stm[:, 0:8], Rt1[:, 0:64].rearrange("p (a e) -> p a e", a=8), axis=AX.X),
        waits=[("SA", tok_e)])
    tok_em = K.e("scalar", lambda a: a.activation(MPtm[:, 0:8], Mtm[:, 0:8], AF.Exp),
                 waits=[("SV", tok_mx)], inc="SA")
    K.e("vector", lambda v: v.reciprocal(sb["Srec"][:, 0:8], Stm[:, 0:8]))
    K.e("vector", lambda v: v.tensor_tensor(MPtm[:, 0:8], MPtm[:, 0:8], sb["Srec"][:, 0:8], ALU.mult),
        waits=[("SA", tok_em)])
    K.e("vector", lambda v: v.tensor_scalar(MPtm[:, 0:8], MPtm[:, 0:8], 0.5, None, ALU.mult))
    for tc in range(ntc):
        K.e("vector", lambda v, t0=tc: v.tensor_scalar(
            Rt2[:, t0 * 8:(t0 + 1) * 8], Ltm[:, t0 * 8:(t0 + 1) * 8],
            Mtm[:, t0:t0 + 1], None, ALU.is_ge))

    def shift_add(src, dst, sh):
        s3 = src[:, 0:64].rearrange("p (a e) -> p a e", a=8)
        d3 = dst[:, 0:64].rearrange("p (a e) -> p a e", a=8)
        K.e("vector", lambda v: v.tensor_copy(d3[:, :, 0:sh], s3[:, :, 0:sh]))
        K.e("vector", lambda v: v.tensor_tensor(d3[:, :, sh:8], s3[:, :, sh:8],
                                                s3[:, :, 0:8 - sh], ALU.add))
    shift_add(Rt2, Rt1, 1)
    shift_add(Rt1, Rt3, 2)
    shift_add(Rt3, Rt1, 4)
    K.e("vector", lambda v: v.tensor_tensor(Rt1[:, 0:64], Rt1[:, 0:64], Rt2[:, 0:64], ALU.subtract))
    K.e("vector", lambda v: v.tensor_scalar(Rt1[:, 0:64], Rt1[:, 0:64], 0.0, None, ALU.is_equal))
    K.e("vector", lambda v: v.tensor_tensor(Rt3[:, 0:64], Rt2[:, 0:64], Rt1[:, 0:64], ALU.mult))
    tok_m = None
    for tc in range(ntc):
        tok_m = K.e("vector", lambda v, t0=tc: v.tensor_scalar(
            Rt3[:, t0 * 8:(t0 + 1) * 8], Rt3[:, t0 * 8:(t0 + 1) * 8],
            MPtm[:, t0:t0 + 1], None, ALU.mult), inc="SV")
    w0 = [("SV", tok_m), K.pwait("PY1"), K._tok_const_g]
    tok_t = K.e("tensor", lambda t: t.transpose(ps["PY1"][0:64, 0:128], Rt3[:, 0:64],
                                                sb["ident"][:, :]),
                waits=w0, inc="SP")
    tok_mt = K.e("vector", lambda v: v.tensor_copy(sb["Msl_t"][:, :], ps["PY1"][0:64, 0:128]),
                 waits=[("SP", tok_t)], inc="SV")
    K.pdone("PY1", ("SV", tok_mt))
    tok_w = K.e("sync", lambda s: s.dma_start(out=P["mseld"][:, :], in_=sb["Msl_t"][:, :]),
                waits=[("SV", tok_mt)], inc="SX", inck=16)
    tok_r2 = K.e("gpsimd", lambda g: g.dma_start(
        out=sb["M8_b"][:, 0:T].rearrange("e (a p) -> e a p", a=8),
        in_=P["mseld"].rearrange("(a e) p -> e a p", e=8)),
        waits=[("SX", tok_w)], inc="SW", inck=16)
    for e in range(NE):
        tok_r2 = K.e("gpsimd", lambda g, ee=e: g.dma_start(
            out=sb["Mrow_b"][0:1, ee * T:(ee + 1) * T].rearrange("q (a p) -> q a p", a=8),
            in_=P["mseld"].rearrange("(a e) p -> e a p", e=8)[ee:ee + 1]),
            inc="SW", inck=16)
    return ("SW", tok_r2)



def _a_compute(K, w1_loads, g, htil, bm, xlnb, T, hraw_free):
    sb, ps = K.sb, K.ps
    ntt = T // 512
    j, hc = divmod(g, 16)
    tok_l, wdst, slot = w1_loads[g]
    pnm = ("PH0", "PH1")[g % 2]
    unit = ps[pnm]
    mt = None
    for tt in range(ntt):
        for kc in range(KC):
            w0 = []
            if tt == 0 and kc == 0:
                w0 = [("SW", tok_l), K.pwait(pnm), K._xlnb_tok]
            mt = K.e("tensor", lambda t, bk=unit[:, tt * 512:(tt + 1) * 512],
                     l=wdst[:, kc * 128:(kc + 1) * 128],
                     r=xlnb[kc][:, tt * 512:(tt + 1) * 512],
                     st=(kc == 0), sp=(kc == KC - 1):
                     t.matmul(bk, l, r, start=st, stop=sp),
                     waits=w0, inc="SP" if kc == KC - 1 else None)
    K.wslot_free[("w1", WR_W1[slot])] = ("SP", mt)
    r = g % 2
    hraw = K.ab(ABF_HRAW + r * T, T)
    wv = [("SP", mt)]
    if hraw_free[r]:
        wv.append(hraw_free[r])
    e_full = K._cur_pair * 2 + j
    tok_r = K.e("scalar", lambda a, d=hraw, u=unit, col=e_full * 16 + hc:
                a.activation(d, u[:, 0:T], AF.Relu, bias=sb["b1_sb"][:, col:col + 1]),
                waits=wv, inc="SA")
    K.pdone(pnm, ("SA", tok_r))
    bmj, bm_tok = bm[j]
    wm = [("SA", tok_r), bm_tok]
    if K._b_done:
        wm.append(K._b_done)
    tok_m = K.e("vector", lambda v, d=htil[j][hc], h=hraw, bb=bmj:
                v.tensor_tensor(d, h, bb, ALU.mult), waits=wm, inc="SV")
    hraw_free[r] = ("SV", tok_m)
    K._a_mask[j] = ("SV", tok_m)
    if K._dbg_slice0 and K._cur_pair == 0 and hc in (0, 8, 15):
        K.dbg(f"HT{j}_{hc}", htil[j][hc], ("SV", tok_m))


def _b_compute(K, loads, mc, pair, htil, xs, T, mrow_tok):
    sb, ps = K.sb, K.ps
    ntt = T // 512
    (tok_l0, wdst0, slot0), (tok_l1, wdst1, slot1) = loads[mc]
    pnm = ("PY0", "PY1")[mc % 2]
    unit = ps[pnm]
    nob2 = os.environ.get("MK_NOB2") == "1"
    mt = None
    for tt in range(ntt):
        w0 = []
        if tt == 0:
            w0 = [("SW", max(tok_l0, tok_l1)), K._a_mask[0], K._a_mask[1], K.pwait(pnm)]
        if not nob2 and pair == 0:
            K.e("tensor", lambda t, bk=unit[:, tt * 512:(tt + 1) * 512],
                l=sb["b2_sb"][0:8, mc * 128:(mc + 1) * 128],
                r=sb["M8_b"][0:8, tt * 512:(tt + 1) * 512]:
                t.matmul(bk, l, r, start=True, stop=False),
                waits=w0 + [mrow_tok])
            w0 = []
        for j, wdst in ((0, wdst0), (1, wdst1)):
            for hc in range(16):
                st = (hc == 0 and j == 0 and (pair != 0 or nob2))
                sp = (hc == 15 and j == 1)
                mt = K.e("tensor", lambda t, bk=unit[:, tt * 512:(tt + 1) * 512],
                         l=wdst[:, hc * 128:(hc + 1) * 128],
                         r=htil[j][hc][:, tt * 512:(tt + 1) * 512],
                         st=st, sp=sp:
                         t.matmul(bk, l, r, start=st, stop=sp),
                         waits=w0 if (hc == 0 and j == 0) else (), inc="SP" if sp else None)
                w0 = []
    K.wslot_free[("w2", WR_W2[slot0])] = ("SP", mt)
    K.wslot_free[("w2", WR_W2[slot1])] = ("SP", mt)
    tk = K.e("vector", lambda v, d=xs[mc], u=unit:
             v.tensor_tensor(d[:, 0:T], u[:, 0:T], d[:, 0:T], ALU.add),
             waits=[("SP", mt)], inc="SV")
    K.pdone(pnm, ("SV", tk))
    if K._dbg_slice0 and mc == 0:
        K.dbg(f"XS{pair}", xs[mc], ("SV", tk))
    return ("SV", tk)


def emit_moe_slice(K, xlnb, xs, T, mrow_tok):
    sb, ps, P = K.sb, K.ps, K.P
    htil = [[K.ab(ABF_HTIL + (ee * 16 + hc) * T, T) for hc in range(16)] for ee in range(2)]
    hraw_free = [None, None]
    bm_free = [None, None]
    final_evac = None

    def load_w1(e, hc, slot):
        src = P["W1s"][e, hc]
        dst = sb["WR"][:, WR_W1[slot]:WR_W1[slot] + 1024]
        return K.e("gpsimd", lambda g: g.dma_start(
            out=dst.rearrange("q (a m) -> q a m", a=8), in_=src),
            waits=[K.wslot_free.get(("w1", WR_W1[slot]))], inc="SW", inck=16), dst

    def load_w2(e, mc, slot):
        src = P["W2s"][e, mc]
        dst = sb["WR"][:, WR_W2[slot]:WR_W2[slot] + 2048]
        return K.e("gpsimd", lambda g: g.dma_start(
            out=dst.rearrange("q (a m) -> q a m", a=16), in_=src),
            waits=[K.wslot_free.get(("w2", WR_W2[slot]))], inc="SW", inck=16), dst

    for pair in range(4):
        K._cur_pair = pair
        bm = []
        for j, e in enumerate((2 * pair, 2 * pair + 1)):
            dstbm = K.ab(ABF_BM + j * T, T)
            pnm = ("PY0", "PY1")[j]
            unit = ps[pnm]
            mt = None
            for tt in range(T // 512):
                w0 = []
                if tt == 0:
                    w0 = [mrow_tok, K.pwait(pnm)]
                mt = K.e("tensor", lambda t, bk=unit[:, tt * 512:(tt + 1) * 512],
                         l=sb["ones_row"][0:1, 0:128],
                         r=sb["Mrow_b"][0:1, e * T + tt * 512:e * T + (tt + 1) * 512]:
                         t.matmul(bk, l, r, start=True, stop=True), waits=w0, inc="SP")
            wv = [("SP", mt)]
            if bm_free[j]:
                wv.append(bm_free[j])
            tk = K.e("vector", lambda v, d=dstbm, u=unit: v.tensor_copy(d, u[:, 0:T]),
                     waits=wv, inc="SV")
            K.pdone(pnm, ("SV", tk))
            if K._cur_pair == 0 and K._dbg_slice0:
                K.dbg(f"BM{j}", dstbm, ("SV", tk))
            bm.append((dstbm, ("SV", tk)))
        w1_loads = {}
        for j in range(2):
            e = 2 * pair + j
            for hc in range(16):
                g = j * 16 + hc
                tok_l, dst = load_w1(e, hc, g % 2)
                w1_loads[g] = (tok_l, dst, g % 2)
                if g >= 1:
                    _a_compute(K, w1_loads, g - 1, htil, bm, xlnb, T, hraw_free)
        _a_compute(K, w1_loads, 31, htil, bm, xlnb, T, hraw_free)
        bm_free[0] = K._a_mask[0]
        bm_free[1] = K._a_mask[1]
        w2_loads = {}
        for mc in range(8):
            s0 = (mc % 2) * 2
            l0 = load_w2(2 * pair + 0, mc, s0)
            l1 = load_w2(2 * pair + 1, mc, s0 + 1)
            w2_loads[mc] = ((l0[0], l0[1], s0), (l1[0], l1[1], s0 + 1))
            if mc >= 1:
                final_evac = _b_compute(K, w2_loads, mc - 1, pair, htil, xs, T, mrow_tok) or final_evac
        final_evac = _b_compute(K, w2_loads, 7, pair, htil, xs, T, mrow_tok) or final_evac
        K._b_done = ("SP", K.b.tok("SP"))
    return final_evac



def emit_preamble(K):
    sb, P = K.sb, K.P
    for nm, dst in (("b1t", "b1_sb"), ("g1", "g1_sb"), ("c1", "c1_sb"),
                    ("g2", "g2_sb"), ("c2", "c2_sb"), ("nonce", "nonce_sb"),
                    ("bo", "bo_sb"), ("bff1", "bff1_sb"), ("bd", "bd_sb")):
        K.e("sync", lambda s, a=P[nm], d=sb[dst]: s.dma_start(out=d[:, :], in_=a[:, :]),
            inc="SX", inck=16)
    tok_bf2 = K.e("sync", lambda s: s.dma_start(out=sb["bff2h_sb"][:, :], in_=P["bff2"][:, :]),
                  inc="SX", inck=16)
    K._tok_const_x = K.b.tok("SX")
    for nm, dst in (("b2", "b2_sb"), ("br", "br_sb")):
        K.e("gpsimd", lambda g, a=P[nm], d=sb[dst]: g.dma_start(out=d[:, :], in_=a[:, :]),
            inc="SW", inck=16)
    K.e("gpsimd", lambda g: g.dma_start(
        out=sb["Wr_sb"][:, :].rearrange("q (a m) -> q a m", a=8),
        in_=P["Wr"].rearrange("a p m -> p a m")),
        inc="SW", inck=16)
    K._tok_const_w = K.b.tok("SW")

    K.e("vector", lambda v: v.memset(sb["ones_row"][:, :], 1.0))
    K.e("vector", lambda v: v.memset(sb["epsrow"][:, :], EPS))
    K.e("vector", lambda v: v.memset(sb["ones_col"][:, :], 1.0))
    tok_mz = K.e("vector", lambda v: v.memset(sb["ident"][:, :], 0.0), inc="SV")
    tok_id = K.e("gpsimd", lambda g: g.affine_select(
        out=sb["ident"][:, :], in_=sb["ident"][:, :], compare_op=ALU.not_equal,
        fill=1.0, base=0, pattern=[[-1, 128]], channel_multiplier=1),
        waits=[("SV", tok_mz)], inc="SG")
    K._tok_const_g = ("SG", tok_id)


def phase_moe(K, which):
    sb, P = K.sb, K.P
    nsl = 4 if which == "moe1" else 2
    last = None
    for sl in range(nsl):
        xs = [K.af(AF_XS + kc * TS, TS) for kc in range(KC)]
        tok_x = None
        for kc in range(KC):
            if which == "moe1":
                src = (P["x_own"] if sl < 2 else P["x_red"])
                a = src[kc * 128:(kc + 1) * 128, (sl % 2) * TS:(sl % 2) * TS + TS]
            else:
                a = P["R3d"][kc, :, sl * TS:(sl + 1) * TS]
            w0 = list(K._xs_free or []) if kc == 0 else []
            if which == "moe2" and kc == 0:
                w0.append(K._r3_tok)
            tok_x = K.e("sync", lambda s, d=xs[kc], aa=a: s.dma_start(out=d, in_=aa),
                        waits=w0, inc="SX", inck=16)
        xlnb = [K.ab(ABF_XLNB + kc * TS, TS) for kc in range(KC)]
        K._xlnb_tok = emit_ln(K, xs, TS, xlnb, xin_tok=("SX", tok_x))
        if sl == 0 and which == "moe1":
            for kc in range(KC):
                K.dbg(f"xlnb{kc}", xlnb[kc], K._xlnb_tok)
        K._dbg_slice0 = (sl == 0 and which == "moe1")
        mrow_tok = emit_router(K, xlnb, TS)
        if sl == 0 and which == "moe1":
            K.dbg("G", K.sb["Rt2"][:, 0:64], mrow_tok)
            K.dbg("MP", K.sb["MPtm"][:, 0:8], mrow_tok)
            K.dbg("SEL", K.sb["Rt3"][:, 0:64], mrow_tok)
            K.dbg("CUM", K.sb["Rt1"][:, 0:64], mrow_tok)
            K.dbg("Mtm", K.sb["Mtm"][:, 0:8], mrow_tok)
            K.dbg("Ltm", K.sb["Ltm"][:, 0:64], mrow_tok)
            K.dbg("Msl", K.sb["Msl_t"][:, :], mrow_tok)
            K.dbg("M8", K.sb["M8_b"][:, 0:TS], mrow_tok)
        last_evac = emit_moe_slice(K, xlnb, xs, TS, mrow_tok)
        tok_sp = None
        for kc in range(KC):
            if which == "moe1":
                ddst = P["R1d"][kc, :, sl * TS:(sl + 1) * TS]
            else:
                ddst = P["y"][kc * 128:(kc + 1) * 128, sl * TS:(sl + 1) * TS]
            tok_sp = K.e("sync", lambda s, a=xs[kc], d=ddst: s.dma_start(out=d, in_=a),
                         waits=[last_evac] if kc == 0 else [], inc="SX", inck=16)
        K._xs_free = [("SX", tok_sp)]
        last = ("SX", tok_sp)
    return last


def emit_end(K, toks):
    K.e("scalar", lambda a: None, waits=toks)


def build_all(stage="full"):
    nc, P, sb, ps, bld = build_program(stage)
    K = Kern(nc, P, sb, ps, bld)
    emit_preamble(K)
    K._castb_free = ("SW", K._tok_const_w)
    K._stat_free = ("SX", K._tok_const_x)
    tok_moe1 = phase_moe(K, "moe1")
    end_toks = [tok_moe1]
    if stage == "moe1":
        tok_d = None
        for kc in range(KC):
            tok_d = K.e("sync", lambda s, a=P["R1d"][kc, :, :],
                        d=P["dbg"][kc * 128:(kc + 1) * 128, :]:
                        s.dma_start(out=d, in_=a), waits=[tok_moe1], inc="SX", inck=16)
        end_toks.append(("SX", tok_d))
    if stage != "moe1":
        emit_p2(K)
        emit_p3(K)
        emit_p4(K)
        emit_p5(K)
        K._b_done = ("SP", K.b.tok("SP"))
        K._castb_free = [("SP", K.b.tok("SP"))]
        tok_y = phase_moe(K, "moe2")
        end_toks = [tok_y]
    emit_end(K, end_toks + K._dbg_toks)
    with nc.Block() as blk:
        blk.tensor(lambda e: bld.replay("tensor", e))
        blk.scalar(lambda e: bld.replay("scalar", e))
        blk.vector(lambda e: bld.replay("vector", e))
        blk.sync(lambda e: bld.replay("sync", e))
        blk.gpsimd(lambda e: bld.replay("gpsimd", e))
    return nc


_BUILD_CACHE = {}
_nonce_ctr = [0]


def kernel(**inputs):
    stage = os.environ.get("MK_STAGE", "full")
    if stage not in _BUILD_CACHE:
        _BUILD_CACHE[stage] = build_all(stage)
    nc = _BUILD_CACHE[stage]
    shared = prep_shared(inputs)
    _nonce_ctr[0] += 1
    nonce = np.array([[0x5A5A0000 + _nonce_ctr[0],
                       np.random.randint(1, 2 ** 31)]], dtype=np.uint32)
    in_maps = []
    for c in range(8):
        m = dict(shared)
        m.update(prep_core(inputs, c))
        m["nonce"] = nonce
        in_maps.append(m)
    res = run_bass_kernel_spmd(nc, in_maps, core_ids=list(range(8)))
    out = np.zeros((2, 8192, D), dtype=np.float32)
    for c in range(8):
        b, own0, _ = core_meta(c)
        out[b, own0:own0 + T_OWN, :] = np.asarray(res.results[c]["y"]).T
    kernel._raw = res.results
    if stage == "moe1":
        kernel._dbg = [np.asarray(res.results[c]["dbg"]) for c in range(8)]
        kernel._dbgx = [{k: np.asarray(v) for k, v in res.results[c].items() if k.startswith("dbg_")}
                        for c in range(8)]
    return out



def emit_p2(K):
    sb, ps, P = K.sb, K.ps, K.P
    wkv = K.ab(ABF_WKV, 16384)
    tok_wk = K.e("gpsimd", lambda g: g.dma_start(
        out=wkv[:, 0:8192].rearrange("q (a m) -> q a m", a=8), in_=P["wk"]),
        inc="SW", inck=16)
    tok_wv = K.e("gpsimd", lambda g: g.dma_start(
        out=wkv[:, 8192:16384].rearrange("q (a m) -> q a m", a=8), in_=P["wv"]),
        inc="SW", inck=16)
    K.dbg("WK", wkv[:, 0:8192], ("SW", tok_wk))
    tok_uz = K.e("vector", lambda v: v.memset(sb["U_sb"][:, :], 0.0))
    for tc in range(8):
        vv = K.ab(ABF_VAUG + tc * 1028, 1028)
        for h in range(NH):
            K.e("vector", lambda v, d=vv[:, h * 257 + 256:h * 257 + 257]: v.memset(d, 1.0))
    tok_ones = K.b.tok("SV")
    kv_done_toks = []
    for sl in range(4):
        own = sl < 2
        xs = [K.af(AF_XS + kc * TS, TS) for kc in range(KC)]
        tok_x = None
        for kc in range(KC):
            w0 = list(K._xs_free or []) if kc == 0 else []
            tok_x = K.e("sync", lambda s, d=xs[kc], a=P["R1d"][kc, :, sl * TS:(sl + 1) * TS]:
                        s.dma_start(out=d, in_=a), waits=w0, inc="SX", inck=16)
        tok_h0ln = emit_ln(K, xs, TS, None, out_f=xs, xin_tok=("SX", tok_x))
        tok_h0 = None
        if own:
            for kc in range(KC):
                tok_h0 = K.e("sync", lambda s, a=xs[kc], d=P["h0d"][kc, :, sl * TS:(sl + 1) * TS]:
                             s.dma_start(out=d, in_=a),
                             waits=[tok_h0ln] if kc == 0 else [], inc="SX", inck=16)
        h1b = [K.ab(ABF_XLNB + kc * TS, TS) for kc in range(KC)]
        tok_h1 = emit_ln(K, xs, TS, h1b, gb=(sb["g1_sb"], sb["c1_sb"]), xin_tok=tok_h0ln)
        tok_h1s = None
        if own:
            for kc in range(KC):
                tok_h1s = K.e("sync", lambda s, a=h1b[kc], d=P["h1d"][kc, :, sl * TS:(sl + 1) * TS]:
                              s.dma_start(out=d, in_=a),
                              waits=[tok_h1] if kc == 0 else [], inc="SX", inck=16)
        for tc in range(8):
            ktm = K.ab(ABF_KTM + tc * TS, TS)
            vv = K.ab(ABF_VAUG + tc * 1028, 1028)
            for which, (pnm, wofs) in enumerate((("PH0", 0), ("PH1", 8192))):
                unit = ps[pnm]
                mt = None
                for cg in range(2):
                    for kc in range(KC):
                        w0 = []
                        if cg == 0 and kc == 0:
                            w0 = [tok_h1, K.pwait(pnm)]
                            if which == 1:
                                w0.append(("SW", tok_wv))
                            else:
                                w0.append(("SW", tok_wk))
                        mt = K.e("tensor", lambda t, bk=unit[:, cg * 512:(cg + 1) * 512],
                                 l=h1b[kc][:, tc * 128:(tc + 1) * 128],
                                 r=wkv[:, wofs + kc * 1024 + cg * 512:wofs + kc * 1024 + (cg + 1) * 512],
                                 st=(kc == 0), sp=(kc == KC - 1):
                                 t.matmul(bk, l, r, start=st, stop=sp),
                                 waits=w0, inc="SP" if kc == KC - 1 else None)
                if which == 0:
                    tk = K.e("scalar", lambda a, d=ktm, u=unit:
                             a.activation(d, u[:, 0:TS], AF.Exp, scale=SCALE),
                             waits=[("SP", mt)], inc="SA")
                    K.pdone(pnm, ("SA", tk))
                    if sl == 3 and tc == 0 and os.environ.get("MK_DBG", "").find("KP0") >= 0:
                        tkp = K.e("vector", lambda v, d=sb["AFa"][:, AF_TMP:AF_TMP + 512],
                                  u=unit[:, 0:512]: v.tensor_copy(d, u),
                                  waits=[("SA", tk)], inc="SV")
                        K.dbg("KP0", sb["AFa"][:, AF_TMP:AF_TMP + 512], ("SV", tkp))
                        K.pdone(pnm, ("SV", tkp))
                else:
                    tk = None
                    for h in range(NH):
                        w0 = [("SP", mt)]
                        if tc == 0 and h == 0 and sl == 0:
                            w0.append(("SV", tok_ones))
                        tk = K.e("vector", lambda v, d=vv[:, h * 257:h * 257 + 256],
                                 u=unit[:, h * 256:(h + 1) * 256]:
                                 v.tensor_copy(d, u), waits=w0, inc="SV")
                    K.pdone(pnm, ("SV", tk))
        for h in range(NH):
            for ds in range(2):
                pnm = ("PY0", "PY1")[(h * 2 + ds) % 2]
                unit = ps[pnm]
                mt = None
                for tc in range(8):
                    ktm = K.ab(ABF_KTM + tc * TS, TS)
                    vv = K.ab(ABF_VAUG + tc * 1028, 1028)
                    w0 = []
                    if tc == 0:
                        w0 = [K.pwait(pnm), ("SA", K.b.tok("SA")), ("SV", K.b.tok("SV"))]
                    mt = K.e("tensor", lambda t, bk=unit[:, 0:257],
                             l=ktm[:, h * 256 + ds * 128:h * 256 + ds * 128 + 128],
                             r=vv[:, h * 257:h * 257 + 257],
                             st=(tc == 0), sp=(tc == 7):
                             t.matmul(bk, l, r, start=st, stop=sp),
                             waits=w0, inc="SP" if tc == 7 else None)
                tk = K.e("vector", lambda v, d=sb["U_sb"][:, (h * 2 + ds) * 257:(h * 2 + ds + 1) * 257],
                         u=unit[:, 0:257]:
                         v.tensor_tensor(d, u, d, ALU.add),
                         waits=[("SP", mt)], inc="SV")
                K.pdone(pnm, ("SV", tk))
        if sl == 3:
            K.dbg("H1R", h1b[0], tok_h1)
            K.dbg("KT0", K.ab(ABF_KTM + 0 * TS, TS), ("SP", K.b.tok("SP")))
            K.dbg("VA0", K.ab(ABF_VAUG + 0 * 1028, 1028), ("SP", K.b.tok("SP")))
        kv_done_toks.append(("SP", K.b.tok("SP")))
        K._xs_free = [("SX", tok_h1s)] if own else [tok_h1]
        K._castb_free = [("SP", K.b.tok("SP"))] + ([("SX", tok_h1s)] if own else [])
    tok_uw = K.e("sync", lambda s: _exch_write_u(K, s),
                 waits=[("SV", K.b.tok("SV"))], inc="SX", inck=16)
    tok_fw = K.e("sync", lambda s: _exch_write_flag(K, s), waits=[("SX", tok_uw)],
                 inc="SX", inck=16)
    K.e("sync", lambda s: _exch_poll(K, s), waits=[("SX", tok_fw)])
    upart = K.af(AF_XS, 2056)
    tok_ur = K.e("sync", lambda s: _exch_read_u(K, s, upart), inc="SX", inck=16)
    tok_us = K.e("vector", lambda v: v.tensor_tensor(sb["U_sb"][:, :], upart, sb["U_sb"][:, :], ALU.add),
                 waits=[("SX", tok_ur)], inc="SV")
    K._xs_free = [("SV", tok_us)]
    cb = K.ab(ABF_HRAW, 2048)
    tok_c = None
    for hd in range(8):
        K.e("vector", lambda v, hd=hd: v.reciprocal(
            sb["Srec"][:, 0:1], sb["U_sb"][:, hd * 257 + 256:hd * 257 + 257]))
        tok_c = K.e("vector", lambda v, hd=hd: v.tensor_scalar(
            cb[:, hd * 256:(hd + 1) * 256], sb["U_sb"][:, hd * 257:hd * 257 + 256],
            sb["Srec"][:, 0:1], None, ALU.mult), inc="SV")
    K._tok_c = ("SV", tok_c)
    K._cb = cb
    K.dbg("U", sb["U_sb"][:, :], ("SV", tok_us))
    K.dbg("H1", P["h1d"][:, :, :], ("SX", K.b.tok("SX")))
    K.dbg("H0", P["h0d"][:, :, :], ("SX", K.b.tok("SX")))


def _exch_write_u(K, s):
    pid = K.nc.partition_id(engines=[s.engine])
    return s.dma_start(out=K.P["shu"][bass.ts(pid % 2, 1), :, :].rearrange("o p c -> (o p) c"),
                       in_=K.sb["U_sb"][:, :])


def _exch_write_flag(K, s):
    pid = K.nc.partition_id(engines=[s.engine])
    return s.dma_start(out=K.P["shf"][bass.ts(pid % 2, 1), :, :].rearrange("o p c -> (o p) c"),
                       in_=K.sb["nonce_sb"][:, :])


def _exch_read_u(K, s, upart):
    pid = K.nc.partition_id(engines=[s.engine])
    return s.dma_start(out=upart,
                       in_=K.P["shu"][bass.ts((pid + 1) % 2, 1), :, :].rearrange("o p c -> (o p) c"))


def _exch_poll(K, s):
    nc = K.nc
    pid = nc.partition_id(engines=[s.engine])
    psem = K.b.semh["PSEM"]
    r_n0 = s.alloc_register("pn0")
    r_n1 = s.alloc_register("pn1")
    r_f0 = s.alloc_register("pf0")
    r_f1 = s.alloc_register("pf1")
    r_d0 = s.alloc_register("pd0")
    r_d1 = s.alloc_register("pd1")
    r_c = s.alloc_register("pc")
    s.load(r_n0, K.sb["nonce_sb"][0:1, 0:1])
    s.load(r_n1, K.sb["nonce_sb"][0:1, 1:2])
    s.reg_mov(r_c, 0)
    flag_src = K.P["shf"][bass.ts((pid + 1) % 2, 1), :, :].rearrange("o p c -> (o p) c")

    def cond():
        s.dma_start(out=K.sb["flag_sb"][:, :], in_=flag_src).then_inc(psem, 16)
        s.reg_add(r_c, r_c, 16)
        s.wait_ge(psem, r_c)
        s.load(r_f0, K.sb["flag_sb"][0:1, 0:1])
        s.load(r_f1, K.sb["flag_sb"][0:1, 1:2])
        s.reg_alu(r_d0, r_f0, r_n0, ALU.bitwise_xor)
        s.reg_alu(r_d1, r_f1, r_n1, ALU.bitwise_xor)
        s.reg_alu(r_d0, r_d0, r_d1, ALU.bitwise_or)
        return r_d0

    with s.While(cond):
        pass
    return None



P3_H1B = 10240
P3_QB = 26624
P3_RQB = ABF_BM
AF_SQ = 0
AF_BQ = [2048, 2560]
AF_HRR = [3072, 3584]
AF_TA = 4096
AF_XB2 = [4608, 5120]
AF_TB = 8192


def stream_lhsT_groups(K, n, src_of, width, slots, compute):
    sb = K.sb
    loads = {}

    def load(g):
        slot = slots[g % len(slots)]
        dst = sb["WR"][:, slot:slot + width]
        src = src_of(g)
        tok = K.e("gpsimd", lambda gg, d=dst, a=src: gg.dma_start(
            out=d.rearrange("q (a m) -> q a m", a=width // 128), in_=a),
            waits=[K.wslot_free.get(("w1", slot))], inc="SW", inck=16)
        loads[g] = (tok, dst, slot)
        return tok

    for g in range(n):
        load(g)
        if g >= 1:
            tok, dst, slot = loads[g - 1]
            mt = compute(g - 1, dst, tok)
            K.wslot_free[("w1", slot)] = ("SP", mt)
    tok, dst, slot = loads[n - 1]
    mt = compute(n - 1, dst, tok)
    K.wslot_free[("w1", slot)] = ("SP", mt)


def emit_p3(K):
    sb, ps, P = K.sb, K.ps, K.P
    ntt = 4
    h1b = [K.ab(P3_H1B + kc * T_OWN, T_OWN) for kc in range(KC)]
    qb = [K.ab(P3_QB + kc * T_OWN, T_OWN) for kc in range(KC)]
    barrier_pe = ("SP", K.b.tok("SP"))
    tok_h1 = None
    for kc in range(KC):
        w0 = [barrier_pe] if kc == 0 else []
        tok_h1 = K.e("sync", lambda s, d=h1b[kc], a=P["h1d"][kc, :, :]:
                     s.dma_start(out=d, in_=a), waits=w0, inc="SX", inck=16)

    def q_compute(mc, wdst, tok_l):
        mt4 = []
        for tt in range(ntt):
            pnm = ("PH0", "PH1")[tt // 2]
            unit = ps[pnm]
            bank = unit[:, (tt % 2) * 512:(tt % 2) * 512 + 512]
            mt = None
            for kc in range(KC):
                w0 = []
                if kc == 0:
                    w0 = [("SW", tok_l)]
                    if tt == 0:
                        w0 += [("SX", tok_h1), K.pwait("PH0"), K.pwait("PH1")]
                mt = K.e("tensor", lambda t, bk=bank, l=wdst[:, kc * 128:(kc + 1) * 128],
                         r=h1b[kc][:, tt * 512:(tt + 1) * 512],
                         st=(kc == 0), sp=(kc == KC - 1):
                         t.matmul(bk, l, r, start=st, stop=sp),
                         waits=w0, inc="SP" if kc == KC - 1 else None)
            mt4.append(mt)
        t0 = K.e("scalar", lambda a, d=qb[mc][:, 0:1024], u=ps["PH0"]:
                 a.activation(d, u[:, 0:1024], AF.Exp, scale=SCALE),
                 waits=[("SP", mt4[1])], inc="SA")
        K.pdone("PH0", ("SA", t0))
        t1 = K.e("scalar", lambda a, d=qb[mc][:, 1024:2048], u=ps["PH1"]:
                 a.activation(d, u[:, 0:1024], AF.Exp, scale=SCALE),
                 waits=[("SP", mt4[3])], inc="SA")
        K.pdone("PH1", ("SA", t1))
        return mt4[3]

    stream_lhsT_groups(K, 8, lambda g: P["wq"][g], 1024, [0, 1024], q_compute)
    tok_qall = ("SA", K.b.tok("SA"))
    sq = sb["AFa"][0:1, AF_SQ:AF_SQ + 2048]
    rqb = sb["ABFa"][0:1, P3_RQB:P3_RQB + 2048]
    attnb = h1b
    tok_attn = None
    for h in range(NH):
        for tt in range(ntt):
            unit = ps["PY0"]
            bank = unit[0:1, (tt % 2) * 512:(tt % 2) * 512 + 512]
            mt = None
            for ds in range(2):
                w0 = []
                if tt == 0 and ds == 0:
                    w0 = [tok_qall, K.pwait("PY0")]
                if tt >= 2 and ds == 0:
                    w0 = [("SV", K._rt_evac[tt % 2])]
                mt = K.e("tensor", lambda t, bk=bank, l=sb["ones_col"][:, :],
                         r=qb[2 * h + ds][:, tt * 512:(tt + 1) * 512],
                         st=(ds == 0), sp=(ds == 1):
                         t.matmul(bk, l, r, start=st, stop=sp),
                         waits=w0, inc="SP" if ds == 1 else None)
            tk = K.e("vector", lambda v, d=sq[0:1, tt * 512:(tt + 1) * 512], u=bank:
                     v.tensor_copy(d, u), waits=[("SP", mt)], inc="SV")
            K._rt_evac[tt % 2] = tk
        K.pdone("PY0", ("SV", tk))
        K.e("vector", lambda v: v.reciprocal(sq, sq))
        tok_rq = K.e("scalar", lambda a: a.copy(rqb, sq), waits=[("SV", K.b.tok("SV"))],
                     inc="SA")
        for tt in range(ntt):
            unit = ps["PY1"]
            bank = unit[:, (tt % 2) * 512:(tt % 2) * 512 + 512]
            w0 = [("SA", tok_rq)]
            if h == 0 and tt == 0:
                w0.append(K.pwait("PY1"))
            if K._bq_free[tt % 2]:
                w0.append(K._bq_free[tt % 2])
            mtb = K.e("tensor", lambda t, bk=bank,
                      r=rqb[0:1, tt * 512:(tt + 1) * 512]:
                      t.matmul(bk, sb["ones_row"][0:1, 0:128], r, start=True, stop=True),
                      waits=w0, inc="SP")
            bq = sb["AFa"][:, AF_BQ[tt % 2]:AF_BQ[tt % 2] + 512]
            tkb = K.e("vector", lambda v, d=bq, u=bank: v.tensor_copy(d, u),
                      waits=[("SP", mtb)], inc="SV")
            for es in range(2):
                pnm = ("PH0", "PH1")[es]
                unit2 = ps[pnm]
                bank2 = unit2[:, (tt % 2) * 512:(tt % 2) * 512 + 512]
                mt = None
                for ds in range(2):
                    w0 = [K._tok_c, K.pwait(pnm)] if (h == 0 and tt == 0 and ds == 0) else []
                    if ds == 0:
                        w0.append(K._ar_free.get((pnm, tt % 2)))
                    mt = K.e("tensor", lambda t, bk=bank2,
                             l=K._cb[:, (h * 2 + ds) * 256 + es * 128:(h * 2 + ds) * 256 + es * 128 + 128],
                             r=qb[2 * h + ds][:, tt * 512:(tt + 1) * 512],
                             st=(ds == 0), sp=(ds == 1):
                             t.matmul(bk, l, r, start=st, stop=sp),
                             waits=w0, inc="SP" if ds == 1 else None)
                tok_attn = K.e("vector", lambda v, d=attnb[h * 2 + es][:, tt * 512:(tt + 1) * 512],
                               u=bank2, b=bq:
                               v.tensor_tensor(d, u, b, ALU.mult),
                               waits=[("SP", mt), ("SV", tkb)], inc="SV")
                K._ar_free[(pnm, tt % 2)] = ("SV", tok_attn)
            K._bq_free[tt % 2] = ("SV", tok_attn)
    tok_attn_all = ("SV", K.b.tok("SV"))
    K.dbg("ATT", attnb[0], tok_attn_all)
    K.dbg("QB", qb[0], tok_qall)
    K.dbg("RQ", sb["AFa"][0:1, AF_SQ:AF_SQ + 2048], tok_attn_all)
    K.pdone("PH0", tok_attn_all)
    K.pdone("PH1", tok_attn_all)
    K.pdone("PY1", tok_attn_all)

    def wo_compute(mc, wdst, tok_l):
        mt = None
        for tt in range(ntt):
            pnm = ("PY0", "PY1")[mc % 2]
            unit = ps[pnm]
            bank = unit[:, (tt % 2) * 512:(tt % 2) * 512 + 512]
            hr = sb["AFa"][:, AF_HRR[tt % 2]:AF_HRR[tt % 2] + 512]
            tok_hr = K.e("sync", lambda s, d=hr, a=P["h0d"][mc, :, tt * 512:(tt + 1) * 512]:
                         s.dma_start(out=d, in_=a),
                         waits=[K._hr_free.get(tt % 2)], inc="SX", inck=16)
            mtg = None
            for kc in range(KC):
                w0 = []
                if kc == 0:
                    w0 = [("SW", tok_l), tok_attn_all, K.pwait(pnm)]
                mtg = K.e("tensor", lambda t, bk=bank, l=wdst[:, kc * 128:(kc + 1) * 128],
                          r=attnb[kc][:, tt * 512:(tt + 1) * 512],
                          st=(kc == 0), sp=(kc == KC - 1):
                          t.matmul(bk, l, r, start=st, stop=sp),
                          waits=w0, inc="SP" if kc == KC - 1 else None)
            ta = sb["AFa"][:, AF_TA:AF_TA + 512]
            t1 = K.e("scalar", lambda a, d=ta, u=bank, col=mc:
                     a.activation(d, u, AF.Identity, bias=sb["bo_sb"][:, col:col + 1]),
                     waits=[("SP", mtg), K._ta_free], inc="SA")
            K.pdone(pnm, ("SA", t1))
            xb = sb["AFa"][:, AF_XB2[tt % 2]:AF_XB2[tt % 2] + 512]
            t2 = K.e("vector", lambda v, d=xb, a=ta, b=hr:
                     v.tensor_tensor(d, a, b, ALU.add),
                     waits=[("SA", t1), ("SX", tok_hr), K._xb_free.get(tt % 2)], inc="SV")
            tok_x2 = K.e("sync", lambda s, a=xb, d=P["x2d"][mc, :, tt * 512:(tt + 1) * 512]:
                         s.dma_start(out=d, in_=a), waits=[("SV", t2)], inc="SX", inck=16)
            K._ta_free = ("SV", t2)
            K._hr_free[tt % 2] = ("SV", t2)
            K._xb_free[tt % 2] = ("SX", tok_x2)
            mt = mtg
        return mt

    stream_lhsT_groups(K, 8, lambda g: P["wo"][g], 1024, [0, 1024], wo_compute)
    K._x2_tok = ("SX", K.b.tok("SX"))
    K.dbg("X2", P["x2d"][:, :, :], K._x2_tok)



def emit_p4(K):
    sb, ps, P = K.sb, K.ps, K.P
    for sl in range(2):
        xs = [K.af(AF_XS + kc * TS, TS) for kc in range(KC)]
        tok_x = None
        for kc in range(KC):
            w0 = []
            if kc == 0:
                w0 = list(K._xs_free or []) + [K._x2_tok]
            tok_x = K.e("sync", lambda s, d=xs[kc], a=P["x2d"][kc, :, sl * TS:(sl + 1) * TS]:
                        s.dma_start(out=d, in_=a), waits=w0, inc="SX", inck=16)
        h2b = [K.ab(ABF_XLNB + kc * TS, TS) for kc in range(KC)]
        tok_h2 = emit_ln(K, xs, TS, h2b, gb=(sb["g2_sb"], sb["c2_sb"]), xin_tok=("SX", tok_x))
        gb = [K.ab(ABF_GB + mc * TS, TS) for mc in range(32)]

        def ff1_compute(mc, wdst, tok_l):
            pnm = ("PH0", "PH1")[mc % 2]
            unit = ps[pnm]
            mt = None
            for tts in range(2):
                bank = unit[:, tts * 512:(tts + 1) * 512]
                for kc in range(KC):
                    w0 = []
                    if kc == 0:
                        w0 = [("SW", tok_l)]
                        if tts == 0:
                            w0 += [tok_h2, K.pwait(pnm)]
                    mt = K.e("tensor", lambda t, bk=bank, l=wdst[:, kc * 128:(kc + 1) * 128],
                             r=h2b[kc][:, tts * 512:(tts + 1) * 512],
                             st=(kc == 0), sp=(kc == KC - 1):
                             t.matmul(bk, l, r, start=st, stop=sp),
                             waits=w0, inc="SP" if kc == KC - 1 else None)
            tg = K.e("scalar", lambda a, d=gb[mc], u=unit, col=mc:
                     a.activation(d, u[:, 0:TS], AF.Gelu_apprx_tanh,
                                  bias=sb["bff1_sb"][:, col:col + 1]),
                     waits=[("SP", mt)], inc="SA")
            K.pdone(pnm, ("SA", tg))
            return mt

        stream_lhsT_groups(K, 32, lambda g: P["ff1"][g], 1024, [0, 1024], ff1_compute)
        tok_gall = ("SA", K.b.tok("SA"))

        loads = {}

        def load_ff2(mc2, half):
            slot = WR_W2[(mc2 % 2) * 2 + half]
            dst = sb["WR"][:, slot:slot + 2048]
            src = P["ff2"][mc2][:, half * 16:(half + 1) * 16, :]
            tok = K.e("gpsimd", lambda g, d=dst, a=src: g.dma_start(
                out=d.rearrange("q (a m) -> q a m", a=16), in_=a),
                waits=[K.wslot_free.get(("w2", slot))], inc="SW", inck=16)
            loads[(mc2, half)] = (tok, dst, slot)

        def ff2_compute(mc2):
            pnm = ("PY0", "PY1")[mc2 % 2]
            unit = ps[pnm]
            mt = None
            for tts in range(2):
                bank = unit[:, tts * 512:(tts + 1) * 512]
                first = True
                for half in range(2):
                    tok_l, wdst, slot = loads[(mc2, half)]
                    for hc in range(16):
                        w0 = []
                        if first:
                            w0 = [("SW", tok_l), tok_gall]
                            if tts == 0:
                                w0.append(K.pwait(pnm))
                        mt = K.e("tensor", lambda t, bk=bank, l=wdst[:, hc * 128:(hc + 1) * 128],
                                 r=gb[half * 16 + hc][:, tts * 512:(tts + 1) * 512],
                                 st=first, sp=(half == 1 and hc == 15):
                                 t.matmul(bk, l, r, start=st, stop=sp),
                                 waits=w0, inc="SP" if (half == 1 and hc == 15) else None)
                        first = False
            tb = sb["AFa"][:, AF_TB:AF_TB + 1024]
            t1 = K.e("scalar", lambda a, d=tb, u=unit, col=mc2:
                     a.activation(d, u[:, 0:TS], AF.Identity,
                                  bias=sb["bff2h_sb"][:, col:col + 1]),
                     waits=[("SP", mt), K._ta_free], inc="SA")
            K.pdone(pnm, ("SA", t1))
            t2 = K.e("vector", lambda v, d=xs[mc2], a=tb:
                     v.tensor_tensor(d[:, 0:TS], a, d[:, 0:TS], ALU.add),
                     waits=[("SA", t1)], inc="SV")
            K._ta_free = ("SV", t2)
            for slot in (loads[(mc2, 0)][2], loads[(mc2, 1)][2]):
                K.wslot_free[("w2", slot)] = ("SP", mt)
            return ("SV", t2)

        last_ev = None
        for mc2 in range(8):
            load_ff2(mc2, 0)
            load_ff2(mc2, 1)
            if mc2 >= 1:
                last_ev = ff2_compute(mc2 - 1)
        last_ev = ff2_compute(7)
        if sl == 0:
            K.dbg("H2", h2b[0], tok_h2)
            K.dbg("GB0", gb[0], tok_gall)
            K.dbg("GB31", gb[31], tok_gall)
        tok_sp = None
        for kc in range(KC):
            tok_sp = K.e("sync", lambda s, a=xs[kc], d=P["x3d"][kc, :, sl * TS:(sl + 1) * TS]:
                         s.dma_start(out=d, in_=a),
                         waits=[("SV", K.b.tok("SV"))] if kc == 0 else [], inc="SX", inck=16)
        K._xs_free = [("SX", tok_sp)]
        K._castb_free = [("SP", K.b.tok("SP")), ("SX", tok_sp)]
    K._x3_tok = ("SX", K.b.tok("SX"))
    K.dbg("X3", K.P["x3d"][:, :, :], K._x3_tok)



def emit_p5(K):
    sb, ps, P = K.sb, K.ps, K.P
    ntt = 4
    xb3 = [K.ab(ABF_XB + kc * T_OWN, T_OWN) for kc in range(KC)]
    barrier_pe = ("SP", K.b.tok("SP"))
    tok_x3 = None
    for kc in range(KC):
        w0 = [K._x3_tok, barrier_pe] if kc == 0 else []
        tok_x3 = K.e("gpsimd", lambda g, d=xb3[kc], a=P["x3d"][kc, :, :]:
                     g.dma_start(out=d, in_=a), waits=w0, inc="SW", inck=16)

    def wd_compute(mc, wdst, tok_l):
        mt = None
        for tt in range(ntt):
            pnm = ("PY0", "PY1")[mc % 2]
            unit = ps[pnm]
            bank = unit[:, (tt % 2) * 512:(tt % 2) * 512 + 512]
            hr = sb["AFa"][:, AF_HRR[tt % 2]:AF_HRR[tt % 2] + 512]
            tok_hr = K.e("sync", lambda s, d=hr, a=P["R1d"][mc, :, tt * 512:(tt + 1) * 512]:
                         s.dma_start(out=d, in_=a),
                         waits=[K._hr_free.get(tt % 2)], inc="SX", inck=16)
            mtg = None
            for kc in range(KC):
                w0 = []
                if kc == 0:
                    w0 = [("SW", max(tok_l, tok_x3)), K.pwait(pnm)]
                mtg = K.e("tensor", lambda t, bk=bank, l=wdst[:, kc * 128:(kc + 1) * 128],
                          r=xb3[kc][:, tt * 512:(tt + 1) * 512],
                          st=(kc == 0), sp=(kc == KC - 1):
                          t.matmul(bk, l, r, start=st, stop=sp),
                          waits=w0, inc="SP" if kc == KC - 1 else None)
            ta = sb["AFa"][:, AF_TA:AF_TA + 512]
            t1 = K.e("scalar", lambda a, d=ta, u=bank, col=mc:
                     a.activation(d, u, AF.Identity, bias=sb["bd_sb"][:, col:col + 1]),
                     waits=[("SP", mtg), K._ta_free], inc="SA")
            K.pdone(pnm, ("SA", t1))
            xb = sb["AFa"][:, AF_XB2[tt % 2]:AF_XB2[tt % 2] + 512]
            t2 = K.e("vector", lambda v, d=xb, a=ta, b=hr:
                     v.tensor_tensor(d, a, b, ALU.add),
                     waits=[("SA", t1), ("SX", tok_hr), K._xb_free.get(tt % 2)], inc="SV")
            tok_r3 = K.e("sync", lambda s, a=xb, d=P["R3d"][mc, :, tt * 512:(tt + 1) * 512]:
                         s.dma_start(out=d, in_=a), waits=[("SV", t2)], inc="SX", inck=16)
            K._ta_free = ("SV", t2)
            K._hr_free[tt % 2] = ("SV", t2)
            K._xb_free[tt % 2] = ("SX", tok_r3)
            mt = mtg
        return mt

    stream_lhsT_groups(K, 8, lambda g: P["wd"][g], 1024, [0, 1024], wd_compute)
    K._r3_tok = ("SX", K.b.tok("SX"))
    K._xs_free = [K._r3_tok]
    K.dbg("R3", P["R3d"][:, :, :], K._r3_tok)


# version 03 (final)
# speedup vs baseline: 1.2943x; 1.2943x over previous
import os
import numpy as np
import concourse.bass as bass
import concourse.mybir as mybir
from concourse.bass_utils import run_bass_kernel_spmd

F32 = mybir.dt.float32
BF16 = mybir.dt.bfloat16
U32 = mybir.dt.uint32
AF = mybir.ActivationFunctionType
ALU = mybir.AluOpType
AX = mybir.AxisListType

D = 1024
NE = 8
HID = 2048
FFH = 4096
NH = 4
DH = 256
T_OWN = 2048
T_EXT = 4096
TS = 1024
EPS = 1e-5
KC = 8
SCALE = 0.25

ENGINES = ("tensor", "scalar", "vector", "sync", "gpsimd")


class B:
    def __init__(self, nc):
        self.nc = nc
        self.ev = {e: [] for e in ENGINES}
        self.semh = {}
        self.cnt = {}
        for name in ("SW", "SX", "SP", "SA", "SV", "SG", "PSEM"):
            self.semh[name] = nc.alloc_semaphore(name)
            self.cnt[name] = 0

    OWN = {"vector": "SV", "scalar": "SA"}

    def emit(self, eng, fn, waits=(), inc=None, inck=1):
        own = self.OWN.get(eng)
        pre = self.cnt[own] if own else 0
        if own and inc is None:
            inc = own
        tok = None
        if inc is not None:
            self.cnt[inc] += inck
            tok = self.cnt[inc]
        for w in waits:
            if w is None:
                continue
            s, v = w
            assert v <= self.cnt[s], f"wait on future {s}:{v} > {self.cnt[s]}"
        ws = tuple(w for w in waits if w)
        if own and pre > 0:
            ws = ((own, pre),) + ws
        self.ev[eng].append((ws, fn, inc, inck))
        return tok

    def replay(self, eng_name, eng):
        sems = self.semh
        hwm = {}
        for (waits, fn, inc, inck) in self.ev[eng_name]:
            for (s, v) in waits:
                if hwm.get(s, -1) < v:
                    eng.wait_ge(sems[s], v)
                    hwm[s] = v
            inst = fn(eng)
            if inc is not None:
                if inst is None:
                    inst = eng.nop()
                inst.then_inc(sems[inc], inck)

    def tok(self, s):
        return self.cnt[s]



def _c(x):
    return np.ascontiguousarray(x)


def prep_shared(inp):
    d = {}
    W1 = np.asarray(inp["L0_ffn_W1"])
    W2 = np.asarray(inp["L0_ffn_W2"])
    d["W1s"] = _c(W1.reshape(NE, 8, 128, 16, 128).transpose(0, 3, 2, 1, 4))
    d["W2s"] = _c(W2.reshape(NE, 16, 128, 8, 128).transpose(0, 3, 2, 1, 4))
    d["Wr"] = _c(np.asarray(inp["L0_ffn_Wr"]).reshape(8, 128, 8))
    d["br"] = _c(np.asarray(inp["L0_ffn_br"]).reshape(1, 8))
    d["b1t"] = _c(np.asarray(inp["L0_ffn_b1"]).reshape(NE, 16, 128).transpose(2, 0, 1).reshape(128, 128))
    d["b2"] = _c(np.asarray(inp["L0_ffn_b2"]))
    d["wk"] = _c(np.asarray(inp["L0_attn_wk"]).reshape(8, 128, D).transpose(1, 0, 2))
    d["wv"] = _c(np.asarray(inp["L0_attn_wv"]).reshape(8, 128, D).transpose(1, 0, 2))
    d["wq"] = _c(np.asarray(inp["L0_attn_wq"]).reshape(8, 128, 8, 128).transpose(2, 1, 0, 3))
    d["wo"] = _c(np.asarray(inp["L0_attn_wo"]).reshape(8, 128, 8, 128).transpose(2, 1, 0, 3))
    d["ff1"] = _c(np.asarray(inp["L0_attn_w_ff1"]).reshape(8, 128, 32, 128).transpose(2, 1, 0, 3))
    d["ff2"] = _c(np.asarray(inp["L0_attn_w_ff2"]).reshape(32, 128, 8, 128).transpose(2, 1, 0, 3))
    d["wd"] = _c(np.asarray(inp["L0_attn_w_d"]).reshape(8, 128, 8, 128).transpose(2, 1, 0, 3))
    d["bo"] = _c(np.asarray(inp["L0_attn_bo"]).reshape(8, 128).T)
    d["bff1"] = _c(np.asarray(inp["L0_attn_b_ff1"]).reshape(32, 128).T)
    d["bff2"] = _c(np.asarray(inp["L0_attn_b_ff2"]).reshape(8, 128).T)
    d["bd"] = _c(np.asarray(inp["L0_attn_b_d"]).reshape(8, 128).T)
    for nm, key in (("g1", "L0_attn_ln1_g"), ("c1", "L0_attn_ln1_b"),
                    ("g2", "L0_attn_ln2_g"), ("c2", "L0_attn_ln2_b")):
        d[nm] = _c(np.asarray(inp[key]).reshape(8, 128).T)
    return d


def core_meta(c):
    b = 0 if c in (0, 1, 4, 5) else 1
    island_base = 0 if c < 4 else 4096
    own0 = island_base + (c & 1) * 2048
    red0 = (4096 - island_base) + (c & 1) * 2048
    return b, own0, red0


def prep_core(inp, c):
    x = np.asarray(inp["L0_attn_x"])
    b, own0, red0 = core_meta(c)
    return {
        "x_own": _c(x[b, own0:own0 + 2048, :].T),
        "x_red": _c(x[b, red0:red0 + 2048, :].T),
    }


AF_XS = 0
AF_BA = 8192
AF_BC = 9216
AF_TMP = 10240
AF_TMP2 = 10752
AF_N = 11072

ABF_XLNB = 0
ABF_SQB = 8192
ABF_HTIL = 10240
ABF_WKV = 10240
ABF_KTM = 26624
ABF_VAUG = 34816
ABF_QB = 10240
ABF_H1S = 26624
ABF_WQO = 28672
ABF_GB = 10240
ABF_XB = 10240
ABF_HRAW = 43104
ABF_BM = 43104 + 2048
ABF_N = 43104 + 2048 + 2048

WR_W1 = [0, 1024]
WR_W2 = [2048, 4096, 6144, 8192]
WR_N = 10240


def build_program(stage):
    nc = bass.Bass("TRN2", target_bir_lowering=False, num_devices=8)
    P = {}

    def din(name, shape, dt=F32):
        P[name] = nc.dram_tensor(name, list(shape), dt, kind="ExternalInput").ap()

    din("x_own", (D, T_OWN)); din("x_red", (D, T_OWN))
    din("W1s", (NE, 16, 128, 8, 128)); din("W2s", (NE, 8, 128, 16, 128))
    din("Wr", (8, 128, 8)); din("br", (1, 8))
    din("b1t", (128, 128)); din("b2", (8, D))
    din("wk", (128, 8, D)); din("wv", (128, 8, D))
    din("wq", (8, 128, 8, 128)); din("wo", (8, 128, 8, 128))
    din("ff1", (32, 128, 8, 128)); din("ff2", (8, 128, 32, 128))
    din("wd", (8, 128, 8, 128))
    din("bo", (128, 8)); din("bff1", (128, 32)); din("bff2", (128, 8)); din("bd", (128, 8))
    for nm in ("g1", "c1", "g2", "c2"):
        din(nm, (128, 8))
    din("nonce", (1, 2), U32)

    P["y"] = nc.dram_tensor("y", [D, T_OWN], F32, kind="ExternalOutput").ap()
    if stage in ("moe1", "p2"):
        P["dbg"] = nc.dram_tensor("dbg", [D, T_EXT], F32, kind="ExternalOutput").ap()

    P["R1d"] = nc.dram_tensor("R1d", [KC, 128, T_EXT], F32, kind="Internal").ap()
    P["h0d"] = nc.dram_tensor("h0d", [KC, 128, T_OWN], F32, kind="Internal").ap()
    P["h1d"] = nc.dram_tensor("h1d", [KC, 128, T_OWN], BF16, kind="Internal").ap()
    P["x2d"] = nc.dram_tensor("x2d", [KC, 128, T_OWN], F32, kind="Internal").ap()
    P["x3d"] = nc.dram_tensor("x3d", [KC, 128, T_OWN], F32, kind="Internal").ap()
    P["R3d"] = nc.dram_tensor("R3d", [KC, 128, T_OWN], F32, kind="Internal").ap()
    P["mseld"] = nc.dram_tensor("mseld", [64, 128], F32, kind="Internal").ap()
    P["shu"] = nc.dram_tensor("shu", [2, 128, 2056], F32, kind="Internal", addr_space="Shared").ap()
    P["shf"] = nc.dram_tensor("shf", [2, 1, 2], U32, kind="Internal", addr_space="Shared").ap()

    sb = {}

    def S(name, shape, dt):
        sb[name] = nc.alloc_sbuf_tensor(name, list(shape), dt)

    S("ones_row", (1, 512), BF16)
    S("ones_col", (128, 1), BF16)
    S("b1_sb", (128, 128), F32)
    S("b2_sb", (8, D), BF16)
    S("br_sb", (1, 8), BF16)
    S("Wr_sb", (128, 64), BF16)
    S("bo_sb", (128, 8), F32); S("bff1_sb", (128, 32), F32)
    S("bff2h_sb", (128, 8), F32); S("bd_sb", (128, 8), F32)
    for nm in ("g1", "c1", "g2", "c2"):
        S(nm + "_sb", (128, 8), F32)
    S("nonce_sb", (1, 2), U32)
    S("flag_sb", (1, 2), U32)
    S("epsrow", (1, 1), F32)
    S("stat", (1, TS), F32); S("stat2", (1, TS), F32)
    S("arow", (1, TS), F32); S("crow", (1, TS), F32)
    S("arow_b", (1, TS), BF16); S("crow_b", (1, TS), BF16)
    S("M8_b", (8, TS), BF16)
    S("Mrow_b", (1, NE * TS), BF16)
    S("Ltm", (128, 64), F32); S("Rt1", (128, 64), F32)
    S("Rt2", (128, 64), F32); S("Rt3", (128, 64), F32)
    S("Mtm", (128, 8), F32); S("Stm", (128, 8), F32); S("MPtm", (128, 8), F32)
    S("Srec", (128, 8), F32)
    S("Msl_t", (64, 128), F32)
    S("ident", (128, 128), F32)
    S("U_sb", (128, 2056), F32)
    S("AFa", (128, AF_N), F32)
    S("ABFa", (128, ABF_N), BF16)
    S("WR", (128, WR_N), BF16)

    ps = {}
    for nm in ("PH0", "PH1", "PY0", "PY1"):
        ps[nm] = nc.alloc_psum_tensor(nm, [128, 1024], F32)
    bld = B(nc)
    return nc, P, sb, ps, bld


class Kern:
    def __init__(self, nc, P, sb, ps, bld):
        self.nc, self.P, self.sb, self.ps, self.b = nc, P, sb, ps, bld
        self.pfree = {nm: None for nm in ("PH0", "PH1", "PY0", "PY1")}
        self.wslot_free = {}
        self._sq_free = [None, None]
        self._castb_free = None
        self._stat_free = None
        self._rt_evac = [None, None]
        self._xlnb_tok = None
        self._a_mask = [None, None]
        self._cur_pair = 0
        self._b_done = None
        self._xs_free = None
        self._r3_tok = None
        self._dbg_toks = []
        self._dbg_slice0 = False
        self._bq_free = {0: None, 1: None}
        self._ar_free = {}
        self._hr_free = {}
        self._xb_free = {}
        self._tok_c = None
        self._ta_free = None
        self._cb = None
        self._x2_tok = None
        self._x3_tok = None

    def e(self, eng, fn, waits=(), inc=None, inck=1):
        return self.b.emit(eng, fn, waits, inc, inck)

    def pwait(self, nm):
        return self.pfree[nm]

    def pdone(self, nm, tok):
        self.pfree[nm] = tok

    def dbg(self, name, ap, wait, dt=None):
        import os as _os
        taps = _os.environ.get("MK_DBG", "").split(",")
        if name not in taps:
            return
        shape = list(ap.shape)
        t = self.nc.dram_tensor("dbg_" + name, shape, dt or ap.dtype, kind="ExternalOutput").ap()
        tok = self.e("sync", lambda s, d=t, a=ap: s.dma_start(out=d, in_=a),
                     waits=[wait], inc="SX", inck=16)
        self._dbg_toks.append(("SX", tok))

    def af(self, c0, w):
        return self.sb["AFa"][:, c0:c0 + w]

    def ab(self, c0, w):
        return self.sb["ABFa"][:, c0:c0 + w]



def emit_ln(K, xin, T, out_b, out_f=None, gb=None, xin_tok=None, outb_off=ABF_XLNB):
    sb, ps = K.sb, K.ps
    ntt = T // 512
    castb = [K.ab(outb_off + kc * T, T) for kc in range(KC)]
    tok_cast = None
    for kc in range(KC):
        w = []
        if kc == 0:
            if xin_tok:
                w.append(xin_tok)
            if K._castb_free:
                w.extend(K._castb_free if isinstance(K._castb_free, list) else [K._castb_free])
        tok_cast = K.e("scalar", lambda a, s=xin[kc], d=castb[kc]: a.copy(d, s),
                       waits=w, inc="SA")
    tok_sumg = tok_sqg = None
    for tt in range(ntt):
        bank = ps["PH0"][0:1, tt * 512:(tt + 1) * 512]
        for kc in range(KC):
            w0 = []
            if kc == 0:
                w0 = [("SA", tok_cast)]
                if tt == 0:
                    w0.append(K.pwait("PH0"))
            tok_sumg = K.e("tensor", lambda t, bk=bank, l=sb["ones_col"][:, :],
                           r=castb[kc][:, tt * 512:(tt + 1) * 512],
                           st=(kc == 0), sp=(kc == KC - 1):
                           t.matmul(bk, l, r, start=st, stop=sp),
                           waits=w0, inc="SP" if kc == KC - 1 else None)
    for tt in range(ntt):
        bank = ps["PH1"][0:1, tt * 512:(tt + 1) * 512]
        for kc in range(KC):
            r = kc % 2
            sq = K.ab(ABF_SQB + r * 1024, T)
            wsq = [K._sq_free[r]] if K._sq_free[r] else []
            tok_sq = K.e("scalar", lambda a, s=xin[kc], d=sq, t0=tt:
                         a.activation(d[:, t0 * 512:(t0 + 1) * 512],
                                      s[:, t0 * 512:(t0 + 1) * 512], AF.Square),
                         waits=wsq, inc="SA")
            w0 = [("SA", tok_sq)]
            if kc == 0 and tt == 0:
                w0.append(K.pwait("PH1"))
            tok_sqg = K.e("tensor", lambda t, bk=bank, l=sb["ones_col"][:, :],
                          r=sq[:, tt * 512:(tt + 1) * 512],
                          st=(kc == 0), sp=(kc == KC - 1):
                          t.matmul(bk, l, r, start=st, stop=sp),
                          waits=w0, inc="SP")
            K._sq_free[r] = ("SP", tok_sqg)
    wst = [("SP", tok_sumg)]
    if K._stat_free:
        wst.append(K._stat_free)
    tok_sum = K.e("vector", lambda v: v.tensor_copy(sb["stat"][:, 0:T], ps["PH0"][0:1, 0:T]),
                  waits=wst, inc="SV")
    K.pdone("PH0", ("SV", tok_sum))
    tok_sq2 = K.e("vector", lambda v: v.tensor_copy(sb["stat2"][:, 0:T], ps["PH1"][0:1, 0:T]),
                  waits=[("SP", tok_sqg)], inc="SV")
    K.pdone("PH1", ("SV", tok_sq2))
    st_, st2, ar, cr = sb["stat"], sb["stat2"], sb["arow"], sb["crow"]
    K.e("vector", lambda v: v.tensor_scalar(st_[:, 0:T], st_[:, 0:T], 1.0 / D, None, ALU.mult))
    K.e("vector", lambda v: v.tensor_scalar(st2[:, 0:T], st2[:, 0:T], 1.0 / D, None, ALU.mult))
    K.e("vector", lambda v: v.tensor_tensor(cr[:, 0:T], st_[:, 0:T], st_[:, 0:T], ALU.mult))
    tok_v = K.e("vector", lambda v: v.tensor_tensor(st2[:, 0:T], st2[:, 0:T], cr[:, 0:T], ALU.subtract),
                inc="SV")
    tok_ln = K.e("scalar", lambda a: a.activation(st2[:, 0:T], st2[:, 0:T], AF.Ln,
                                                  bias=sb["epsrow"][0:1, 0:1]),
                 waits=[("SV", tok_v)], inc="SA")
    tok_rs = K.e("scalar", lambda a: a.activation(ar[:, 0:T], st2[:, 0:T], AF.Exp, scale=-0.5),
                 waits=[("SA", tok_ln)], inc="SA")
    K.e("vector", lambda v: v.tensor_scalar(st_[:, 0:T], st_[:, 0:T], -1.0, None, ALU.mult),
        waits=[("SA", tok_rs)])
    tok_c = K.e("vector", lambda v: v.tensor_tensor(cr[:, 0:T], st_[:, 0:T], ar[:, 0:T], ALU.mult),
                inc="SV")
    K.e("scalar", lambda a: a.copy(sb["arow_b"][:, 0:T], ar[:, 0:T]), waits=[("SV", tok_c)])
    tok_cb = K.e("scalar", lambda a: a.copy(sb["crow_b"][:, 0:T], cr[:, 0:T]), inc="SA")
    ba = K.af(AF_BA, T)
    bc = K.af(AF_BC, T)
    tok_bab = None
    for pnm, row, dst in (("PY0", sb["arow_b"], ba), ("PY1", sb["crow_b"], bc)):
        unit = ps[pnm]
        mt = None
        for tt in range(ntt):
            w0 = []
            if tt == 0:
                w0 = [("SA", tok_cb), K.pwait(pnm)]
            mt = K.e("tensor", lambda t, bk=unit[:, tt * 512:(tt + 1) * 512],
                     l=sb["ones_row"][0:1, 0:128], r=row[0:1, tt * 512:(tt + 1) * 512]:
                     t.matmul(bk, l, r, start=True, stop=True),
                     waits=w0, inc="SP")
        tok_bab = K.e("vector", lambda v, d=dst, u=unit: v.tensor_copy(d, u[:, 0:T]),
                      waits=[("SP", mt)], inc="SV")
        K.pdone(pnm, ("SV", tok_bab))
    tmp = K.af(AF_TMP, 512)
    tok_out = None
    prev_act = None
    for kc in range(KC):
        for tt in range(ntt):
            sl = slice(tt * 512, (tt + 1) * 512)
            w1 = []
            if kc == 0 and tt == 0:
                w1.append(("SV", tok_bab))
            if prev_act is not None:
                w1.append(("SA", prev_act))
                prev_act = None
            K.e("vector", lambda v, a=xin[kc], s=sl:
                v.tensor_tensor(tmp, a[:, s], ba[:, s], ALU.mult), waits=w1)
            if gb is None:
                if out_f is not None:
                    t2 = K.e("vector", lambda v, d=out_f[kc], s=sl:
                             v.tensor_tensor(d[:, s], tmp, bc[:, s], ALU.add), inc="SV")
                    tok_out = ("SV", t2)
                    if out_b is not None:
                        t3 = K.e("scalar", lambda a, d=out_b[kc], sf=out_f[kc], s=sl:
                                 a.copy(d[:, s], sf[:, s]), waits=[("SV", t2)], inc="SA")
                        tok_out = ("SA", t3)
                else:
                    t2 = K.e("vector", lambda v, d=out_b[kc], s=sl:
                             v.tensor_tensor(d[:, s], tmp, bc[:, s], ALU.add), inc="SV")
                    tok_out = ("SV", t2)
            else:
                g_ap, b_ap = gb
                t2 = K.e("vector", lambda v, s=sl:
                         v.tensor_tensor(tmp, tmp, bc[:, s], ALU.add), inc="SV")
                t3 = K.e("scalar", lambda a, d=out_b[kc], s=sl, kk=kc:
                         a.activation(d[:, s], tmp, AF.Identity,
                                      scale=g_ap[:, kk:kk + 1], bias=b_ap[:, kk:kk + 1]),
                         waits=[("SV", t2)], inc="SA")
                prev_act = t3
                tok_out = ("SA", t3)
    K._castb_free = ("SP", tok_sumg)
    K._stat_free = ("SA", tok_cb)
    return tok_out



def emit_router(K, xlnb, T):
    sb, ps, P = K.sb, K.ps, K.P
    ntc = T // 128
    unit = ps["PY0"]
    tok_l = None
    for tc in range(ntc):
        half = tc % 2
        view = unit[:, half * 512: half * 512 + 8]
        w0 = []
        if tc == 0:
            w0 = [K._xlnb_tok]
        if tc < 2:
            w0.append(K.pwait("PY0"))
        else:
            w0.append(("SV", K._rt_evac[half]))
        K.e("tensor", lambda t, bk=view: t.matmul(
            bk, sb["ones_row"][0:1, 0:128], sb["br_sb"][0:1, :], start=True, stop=False),
            waits=w0)
        mt = None
        for kc in range(KC):
            mt = K.e("tensor", lambda t, bk=view, l=xlnb[kc][:, tc * 128:(tc + 1) * 128],
                     r=sb["Wr_sb"][:, kc * 8:(kc + 1) * 8], sp=(kc == KC - 1):
                     t.matmul(bk, l, r, start=False, stop=sp),
                     inc="SP" if kc == KC - 1 else None)
        tok_l = K.e("vector", lambda v, d=sb["Ltm"][:, tc * 8:(tc + 1) * 8], u=view:
                    v.tensor_copy(d, u), waits=[("SP", mt)], inc="SV")
        K._rt_evac[half] = tok_l
    K.pdone("PY0", ("SV", tok_l))
    Ltm, Rt1, Rt2, Rt3 = sb["Ltm"], sb["Rt1"], sb["Rt2"], sb["Rt3"]
    Mtm, Stm, MPtm = sb["Mtm"], sb["Stm"], sb["MPtm"]
    tok_mx = K.e("vector", lambda v: v.reduce_max(
        Mtm[:, 0:8], Ltm[:, 0:64].rearrange("p (a e) -> p a e", a=8), axis=AX.X), inc="SV")
    tok_e = K.e("scalar", lambda a: a.activation(Rt1[:, 0:64], Ltm[:, 0:64], AF.Exp),
                waits=[("SV", tok_l)], inc="SA")
    K.e("vector", lambda v: v.reduce_sum(
        Stm[:, 0:8], Rt1[:, 0:64].rearrange("p (a e) -> p a e", a=8), axis=AX.X),
        waits=[("SA", tok_e)])
    tok_em = K.e("scalar", lambda a: a.activation(MPtm[:, 0:8], Mtm[:, 0:8], AF.Exp),
                 waits=[("SV", tok_mx)], inc="SA")
    K.e("vector", lambda v: v.reciprocal(sb["Srec"][:, 0:8], Stm[:, 0:8]))
    K.e("vector", lambda v: v.tensor_tensor(MPtm[:, 0:8], MPtm[:, 0:8], sb["Srec"][:, 0:8], ALU.mult),
        waits=[("SA", tok_em)])
    K.e("vector", lambda v: v.tensor_scalar(MPtm[:, 0:8], MPtm[:, 0:8], 0.5, None, ALU.mult))
    for tc in range(ntc):
        K.e("vector", lambda v, t0=tc: v.tensor_scalar(
            Rt2[:, t0 * 8:(t0 + 1) * 8], Ltm[:, t0 * 8:(t0 + 1) * 8],
            Mtm[:, t0:t0 + 1], None, ALU.is_ge))

    def shift_add(src, dst, sh):
        s3 = src[:, 0:64].rearrange("p (a e) -> p a e", a=8)
        d3 = dst[:, 0:64].rearrange("p (a e) -> p a e", a=8)
        K.e("vector", lambda v: v.tensor_copy(d3[:, :, 0:sh], s3[:, :, 0:sh]))
        K.e("vector", lambda v: v.tensor_tensor(d3[:, :, sh:8], s3[:, :, sh:8],
                                                s3[:, :, 0:8 - sh], ALU.add))
    shift_add(Rt2, Rt1, 1)
    shift_add(Rt1, Rt3, 2)
    shift_add(Rt3, Rt1, 4)
    K.e("vector", lambda v: v.tensor_tensor(Rt1[:, 0:64], Rt1[:, 0:64], Rt2[:, 0:64], ALU.subtract))
    K.e("vector", lambda v: v.tensor_scalar(Rt1[:, 0:64], Rt1[:, 0:64], 0.0, None, ALU.is_equal))
    K.e("vector", lambda v: v.tensor_tensor(Rt3[:, 0:64], Rt2[:, 0:64], Rt1[:, 0:64], ALU.mult))
    tok_m = None
    for tc in range(ntc):
        tok_m = K.e("vector", lambda v, t0=tc: v.tensor_scalar(
            Rt3[:, t0 * 8:(t0 + 1) * 8], Rt3[:, t0 * 8:(t0 + 1) * 8],
            MPtm[:, t0:t0 + 1], None, ALU.mult), inc="SV")
    w0 = [("SV", tok_m), K.pwait("PY1"), K._tok_const_g]
    tok_t = K.e("tensor", lambda t: t.transpose(ps["PY1"][0:64, 0:128], Rt3[:, 0:64],
                                                sb["ident"][:, :]),
                waits=w0, inc="SP")
    tok_mt = K.e("vector", lambda v: v.tensor_copy(sb["Msl_t"][:, :], ps["PY1"][0:64, 0:128]),
                 waits=[("SP", tok_t)], inc="SV")
    K.pdone("PY1", ("SV", tok_mt))
    tok_w = K.e("sync", lambda s: s.dma_start(out=P["mseld"][:, :], in_=sb["Msl_t"][:, :]),
                waits=[("SV", tok_mt)], inc="SX", inck=16)
    tok_r2 = K.e("gpsimd", lambda g: g.dma_start(
        out=sb["M8_b"][:, 0:T].rearrange("e (a p) -> e a p", a=8),
        in_=P["mseld"].rearrange("(a e) p -> e a p", e=8)),
        waits=[("SX", tok_w)], inc="SW", inck=16)
    for e in range(NE):
        tok_r2 = K.e("gpsimd", lambda g, ee=e: g.dma_start(
            out=sb["Mrow_b"][0:1, ee * T:(ee + 1) * T].rearrange("q (a p) -> q a p", a=8),
            in_=P["mseld"].rearrange("(a e) p -> e a p", e=8)[ee:ee + 1]),
            inc="SW", inck=16)
    return ("SW", tok_r2)



def _a_compute(K, w1_loads, g, htil, bm, xlnb, T, hraw_free):
    sb, ps = K.sb, K.ps
    ntt = T // 512
    j, hc = divmod(g, 16)
    tok_l, wdst, slot = w1_loads[g]
    pnm = ("PH0", "PH1")[g % 2]
    unit = ps[pnm]
    mt = None
    for tt in range(ntt):
        for kc in range(KC):
            w0 = []
            if tt == 0 and kc == 0:
                w0 = [("SW", tok_l), K.pwait(pnm), K._xlnb_tok]
            mt = K.e("tensor", lambda t, bk=unit[:, tt * 512:(tt + 1) * 512],
                     l=wdst[:, kc * 128:(kc + 1) * 128],
                     r=xlnb[kc][:, tt * 512:(tt + 1) * 512],
                     st=(kc == 0), sp=(kc == KC - 1):
                     t.matmul(bk, l, r, start=st, stop=sp),
                     waits=w0, inc="SP" if kc == KC - 1 else None)
    K.wslot_free[("w1", WR_W1[slot])] = ("SP", mt)
    r = g % 2
    hraw = K.ab(ABF_HRAW + r * T, T)
    wv = [("SP", mt)]
    if hraw_free[r]:
        wv.append(hraw_free[r])
    e_full = K._cur_pair * 2 + j
    tok_r = K.e("scalar", lambda a, d=hraw, u=unit, col=e_full * 16 + hc:
                a.activation(d, u[:, 0:T], AF.Relu, bias=sb["b1_sb"][:, col:col + 1]),
                waits=wv, inc="SA")
    K.pdone(pnm, ("SA", tok_r))
    bmj, bm_tok = bm[j]
    wm = [("SA", tok_r), bm_tok]
    if K._b_done:
        wm.append(K._b_done)
    tok_m = K.e("vector", lambda v, d=htil[j][hc], h=hraw, bb=bmj:
                v.tensor_tensor(d, h, bb, ALU.mult), waits=wm, inc="SV")
    hraw_free[r] = ("SV", tok_m)
    K._a_mask[j] = ("SV", tok_m)
    if K._dbg_slice0 and K._cur_pair == 0 and hc in (0, 8, 15):
        K.dbg(f"HT{j}_{hc}", htil[j][hc], ("SV", tok_m))


def _b_compute(K, loads, mc, pair, htil, xs, T, mrow_tok):
    sb, ps = K.sb, K.ps
    ntt = T // 512
    (tok_l0, wdst0, slot0), (tok_l1, wdst1, slot1) = loads[mc]
    pnm = ("PY0", "PY1")[mc % 2]
    unit = ps[pnm]
    nob2 = os.environ.get("MK_NOB2") == "1"
    mt = None
    for tt in range(ntt):
        w0 = []
        if tt == 0:
            w0 = [("SW", max(tok_l0, tok_l1)), K._a_mask[0], K._a_mask[1], K.pwait(pnm)]
        if not nob2 and pair == 0:
            K.e("tensor", lambda t, bk=unit[:, tt * 512:(tt + 1) * 512],
                l=sb["b2_sb"][0:8, mc * 128:(mc + 1) * 128],
                r=sb["M8_b"][0:8, tt * 512:(tt + 1) * 512]:
                t.matmul(bk, l, r, start=True, stop=False),
                waits=w0 + [mrow_tok])
            w0 = []
        for j, wdst in ((0, wdst0), (1, wdst1)):
            for hc in range(16):
                st = (hc == 0 and j == 0 and (pair != 0 or nob2))
                sp = (hc == 15 and j == 1)
                mt = K.e("tensor", lambda t, bk=unit[:, tt * 512:(tt + 1) * 512],
                         l=wdst[:, hc * 128:(hc + 1) * 128],
                         r=htil[j][hc][:, tt * 512:(tt + 1) * 512],
                         st=st, sp=sp:
                         t.matmul(bk, l, r, start=st, stop=sp),
                         waits=w0 if (hc == 0 and j == 0) else (), inc="SP" if sp else None)
                w0 = []
    K.wslot_free[("w2", WR_W2[slot0])] = ("SP", mt)
    K.wslot_free[("w2", WR_W2[slot1])] = ("SP", mt)
    tk = K.e("vector", lambda v, d=xs[mc], u=unit:
             v.tensor_tensor(d[:, 0:T], u[:, 0:T], d[:, 0:T], ALU.add),
             waits=[("SP", mt)], inc="SV")
    K.pdone(pnm, ("SV", tk))
    if K._dbg_slice0 and mc == 0:
        K.dbg(f"XS{pair}", xs[mc], ("SV", tk))
    return ("SV", tk)


def emit_moe_slice(K, xlnb, xs, T, mrow_tok):
    sb, ps, P = K.sb, K.ps, K.P
    htil = [[K.ab(ABF_HTIL + (ee * 16 + hc) * T, T) for hc in range(16)] for ee in range(2)]
    hraw_free = [None, None]
    bm_free = [None, None]
    final_evac = None

    def load_w1(e, hc, slot):
        src = P["W1s"][e, hc]
        dst = sb["WR"][:, WR_W1[slot]:WR_W1[slot] + 1024]
        return K.e("gpsimd", lambda g: g.dma_start(
            out=dst.rearrange("q (a m) -> q a m", a=8), in_=src),
            waits=[K.wslot_free.get(("w1", WR_W1[slot]))], inc="SW", inck=16), dst

    def load_w2(e, mc, slot):
        src = P["W2s"][e, mc]
        dst = sb["WR"][:, WR_W2[slot]:WR_W2[slot] + 2048]
        return K.e("gpsimd", lambda g: g.dma_start(
            out=dst.rearrange("q (a m) -> q a m", a=16), in_=src),
            waits=[K.wslot_free.get(("w2", WR_W2[slot]))], inc="SW", inck=16), dst

    for pair in range(4):
        K._cur_pair = pair
        bm = []
        for j, e in enumerate((2 * pair, 2 * pair + 1)):
            dstbm = K.ab(ABF_BM + j * T, T)
            pnm = ("PY0", "PY1")[j]
            unit = ps[pnm]
            mt = None
            for tt in range(T // 512):
                w0 = []
                if tt == 0:
                    w0 = [mrow_tok, K.pwait(pnm)]
                mt = K.e("tensor", lambda t, bk=unit[:, tt * 512:(tt + 1) * 512],
                         l=sb["ones_row"][0:1, 0:128],
                         r=sb["Mrow_b"][0:1, e * T + tt * 512:e * T + (tt + 1) * 512]:
                         t.matmul(bk, l, r, start=True, stop=True), waits=w0, inc="SP")
            wv = [("SP", mt)]
            if bm_free[j]:
                wv.append(bm_free[j])
            tk = K.e("vector", lambda v, d=dstbm, u=unit: v.tensor_copy(d, u[:, 0:T]),
                     waits=wv, inc="SV")
            K.pdone(pnm, ("SV", tk))
            if K._cur_pair == 0 and K._dbg_slice0:
                K.dbg(f"BM{j}", dstbm, ("SV", tk))
            bm.append((dstbm, ("SV", tk)))
        w1_loads = {}
        for j in range(2):
            e = 2 * pair + j
            for hc in range(16):
                g = j * 16 + hc
                tok_l, dst = load_w1(e, hc, g % 2)
                w1_loads[g] = (tok_l, dst, g % 2)
                if g >= 1:
                    _a_compute(K, w1_loads, g - 1, htil, bm, xlnb, T, hraw_free)
        _a_compute(K, w1_loads, 31, htil, bm, xlnb, T, hraw_free)
        bm_free[0] = K._a_mask[0]
        bm_free[1] = K._a_mask[1]
        w2_loads = {}
        for mc in range(8):
            s0 = (mc % 2) * 2
            l0 = load_w2(2 * pair + 0, mc, s0)
            l1 = load_w2(2 * pair + 1, mc, s0 + 1)
            w2_loads[mc] = ((l0[0], l0[1], s0), (l1[0], l1[1], s0 + 1))
            if mc >= 1:
                final_evac = _b_compute(K, w2_loads, mc - 1, pair, htil, xs, T, mrow_tok) or final_evac
        final_evac = _b_compute(K, w2_loads, 7, pair, htil, xs, T, mrow_tok) or final_evac
        K._b_done = ("SP", K.b.tok("SP"))
    return final_evac



def emit_preamble(K):
    sb, P = K.sb, K.P
    for nm, dst in (("b1t", "b1_sb"), ("g1", "g1_sb"), ("c1", "c1_sb"),
                    ("g2", "g2_sb"), ("c2", "c2_sb"), ("nonce", "nonce_sb"),
                    ("bo", "bo_sb"), ("bff1", "bff1_sb"), ("bd", "bd_sb")):
        K.e("sync", lambda s, a=P[nm], d=sb[dst]: s.dma_start(out=d[:, :], in_=a[:, :]),
            inc="SX", inck=16)
    tok_bf2 = K.e("sync", lambda s: s.dma_start(out=sb["bff2h_sb"][:, :], in_=P["bff2"][:, :]),
                  inc="SX", inck=16)
    K._tok_const_x = K.b.tok("SX")
    for nm, dst in (("b2", "b2_sb"), ("br", "br_sb")):
        K.e("gpsimd", lambda g, a=P[nm], d=sb[dst]: g.dma_start(out=d[:, :], in_=a[:, :]),
            inc="SW", inck=16)
    K.e("gpsimd", lambda g: g.dma_start(
        out=sb["Wr_sb"][:, :].rearrange("q (a m) -> q a m", a=8),
        in_=P["Wr"].rearrange("a p m -> p a m")),
        inc="SW", inck=16)
    K._tok_const_w = K.b.tok("SW")

    K.e("vector", lambda v: v.memset(sb["ones_row"][:, :], 1.0))
    K.e("vector", lambda v: v.memset(sb["epsrow"][:, :], EPS))
    K.e("vector", lambda v: v.memset(sb["ones_col"][:, :], 1.0))
    tok_mz = K.e("vector", lambda v: v.memset(sb["ident"][:, :], 0.0), inc="SV")
    K._tok_memset = ("SV", tok_mz)
    tok_id = K.e("gpsimd", lambda g: g.affine_select(
        out=sb["ident"][:, :], in_=sb["ident"][:, :], compare_op=ALU.not_equal,
        fill=1.0, base=0, pattern=[[-1, 128]], channel_multiplier=1),
        waits=[("SV", tok_mz)], inc="SG")
    K._tok_const_g = ("SG", tok_id)
    K.e("tensor", lambda t: None,
        waits=[K._tok_memset, ("SX", K._tok_const_x), ("SW", K._tok_const_w)])


def phase_moe(K, which):
    sb, P = K.sb, K.P
    nsl = 4 if which == "moe1" else 2
    last = None
    for sl in range(nsl):
        xs = [K.af(AF_XS + kc * TS, TS) for kc in range(KC)]
        tok_x = None
        for kc in range(KC):
            if which == "moe1":
                src = (P["x_own"] if sl < 2 else P["x_red"])
                a = src[kc * 128:(kc + 1) * 128, (sl % 2) * TS:(sl % 2) * TS + TS]
            else:
                a = P["R3d"][kc, :, sl * TS:(sl + 1) * TS]
            w0 = list(K._xs_free or []) if kc == 0 else []
            if which == "moe2" and kc == 0:
                w0.append(K._r3_tok)
            tok_x = K.e("sync", lambda s, d=xs[kc], aa=a: s.dma_start(out=d, in_=aa),
                        waits=w0, inc="SX", inck=16)
        xlnb = [K.ab(ABF_XLNB + kc * TS, TS) for kc in range(KC)]
        K._xlnb_tok = emit_ln(K, xs, TS, xlnb, xin_tok=("SX", tok_x))
        if sl == 0 and which == "moe1":
            for kc in range(KC):
                K.dbg(f"xlnb{kc}", xlnb[kc], K._xlnb_tok)
        K._dbg_slice0 = (sl == 0 and which == "moe1")
        mrow_tok = emit_router(K, xlnb, TS)
        if sl == 0 and which == "moe1":
            K.dbg("G", K.sb["Rt2"][:, 0:64], mrow_tok)
            K.dbg("MP", K.sb["MPtm"][:, 0:8], mrow_tok)
            K.dbg("SEL", K.sb["Rt3"][:, 0:64], mrow_tok)
            K.dbg("CUM", K.sb["Rt1"][:, 0:64], mrow_tok)
            K.dbg("Mtm", K.sb["Mtm"][:, 0:8], mrow_tok)
            K.dbg("Ltm", K.sb["Ltm"][:, 0:64], mrow_tok)
            K.dbg("Msl", K.sb["Msl_t"][:, :], mrow_tok)
            K.dbg("M8", K.sb["M8_b"][:, 0:TS], mrow_tok)
        last_evac = emit_moe_slice(K, xlnb, xs, TS, mrow_tok)
        tok_sp = None
        for kc in range(KC):
            if which == "moe1":
                ddst = P["R1d"][kc, :, sl * TS:(sl + 1) * TS]
            else:
                ddst = P["y"][kc * 128:(kc + 1) * 128, sl * TS:(sl + 1) * TS]
            tok_sp = K.e("sync", lambda s, a=xs[kc], d=ddst: s.dma_start(out=d, in_=a),
                         waits=[last_evac] if kc == 0 else [], inc="SX", inck=16)
        K._xs_free = [("SX", tok_sp)]
        last = ("SX", tok_sp)
    return last


def emit_end(K, toks):
    K.e("scalar", lambda a: None, waits=toks)


def build_all(stage="full"):
    nc, P, sb, ps, bld = build_program(stage)
    K = Kern(nc, P, sb, ps, bld)
    emit_preamble(K)
    K._castb_free = ("SW", K._tok_const_w)
    K._stat_free = ("SX", K._tok_const_x)
    tok_moe1 = phase_moe(K, "moe1")
    end_toks = [tok_moe1]
    if stage == "moe1":
        tok_d = None
        for kc in range(KC):
            tok_d = K.e("sync", lambda s, a=P["R1d"][kc, :, :],
                        d=P["dbg"][kc * 128:(kc + 1) * 128, :]:
                        s.dma_start(out=d, in_=a), waits=[tok_moe1], inc="SX", inck=16)
        end_toks.append(("SX", tok_d))
    if stage != "moe1":
        emit_p2(K)
        emit_p3(K)
        emit_p4(K)
        emit_p5(K)
        K._b_done = ("SP", K.b.tok("SP"))
        K._castb_free = [("SP", K.b.tok("SP"))]
        tok_y = phase_moe(K, "moe2")
        end_toks = [tok_y]
    emit_end(K, end_toks + K._dbg_toks)
    with nc.Block() as blk:
        blk.tensor(lambda e: bld.replay("tensor", e))
        blk.scalar(lambda e: bld.replay("scalar", e))
        blk.vector(lambda e: bld.replay("vector", e))
        blk.sync(lambda e: bld.replay("sync", e))
        blk.gpsimd(lambda e: bld.replay("gpsimd", e))
    return nc


_BUILD_CACHE = {}
_nonce_ctr = [0]


def kernel(**inputs):
    stage = os.environ.get("MK_STAGE", "full")
    if stage not in _BUILD_CACHE:
        _BUILD_CACHE[stage] = build_all(stage)
    nc = _BUILD_CACHE[stage]
    shared = prep_shared(inputs)
    _nonce_ctr[0] += 1
    nonce = np.array([[0x5A5A0000 + _nonce_ctr[0],
                       np.random.randint(1, 2 ** 31)]], dtype=np.uint32)
    in_maps = []
    for c in range(8):
        m = dict(shared)
        m.update(prep_core(inputs, c))
        m["nonce"] = nonce
        in_maps.append(m)
    res = run_bass_kernel_spmd(nc, in_maps, core_ids=list(range(8)))
    out = np.zeros((2, 8192, D), dtype=np.float32)
    for c in range(8):
        b, own0, _ = core_meta(c)
        out[b, own0:own0 + T_OWN, :] = np.asarray(res.results[c]["y"]).T
    kernel._raw = res.results
    if stage == "moe1":
        kernel._dbg = [np.asarray(res.results[c]["dbg"]) for c in range(8)]
        kernel._dbgx = [{k: np.asarray(v) for k, v in res.results[c].items() if k.startswith("dbg_")}
                        for c in range(8)]
    return out



def emit_p2(K):
    sb, ps, P = K.sb, K.ps, K.P
    wkv = K.ab(ABF_WKV, 16384)
    tok_wk = K.e("gpsimd", lambda g: g.dma_start(
        out=wkv[:, 0:8192].rearrange("q (a m) -> q a m", a=8), in_=P["wk"]),
        inc="SW", inck=16)
    tok_wv = K.e("gpsimd", lambda g: g.dma_start(
        out=wkv[:, 8192:16384].rearrange("q (a m) -> q a m", a=8), in_=P["wv"]),
        inc="SW", inck=16)
    K.dbg("WK", wkv[:, 0:8192], ("SW", tok_wk))
    tok_uz = K.e("vector", lambda v: v.memset(sb["U_sb"][:, :], 0.0))
    for tc in range(8):
        vv = K.ab(ABF_VAUG + tc * 1028, 1028)
        for h in range(NH):
            K.e("vector", lambda v, d=vv[:, h * 257 + 256:h * 257 + 257]: v.memset(d, 1.0))
    tok_ones = K.b.tok("SV")
    kv_done_toks = []
    for sl in range(4):
        own = sl < 2
        xs = [K.af(AF_XS + kc * TS, TS) for kc in range(KC)]
        tok_x = None
        for kc in range(KC):
            w0 = list(K._xs_free or []) if kc == 0 else []
            tok_x = K.e("sync", lambda s, d=xs[kc], a=P["R1d"][kc, :, sl * TS:(sl + 1) * TS]:
                        s.dma_start(out=d, in_=a), waits=w0, inc="SX", inck=16)
        tok_h0ln = emit_ln(K, xs, TS, None, out_f=xs, xin_tok=("SX", tok_x))
        tok_h0 = None
        if own:
            for kc in range(KC):
                tok_h0 = K.e("sync", lambda s, a=xs[kc], d=P["h0d"][kc, :, sl * TS:(sl + 1) * TS]:
                             s.dma_start(out=d, in_=a),
                             waits=[tok_h0ln] if kc == 0 else [], inc="SX", inck=16)
        h1b = [K.ab(ABF_XLNB + kc * TS, TS) for kc in range(KC)]
        tok_h1 = emit_ln(K, xs, TS, h1b, gb=(sb["g1_sb"], sb["c1_sb"]), xin_tok=tok_h0ln)
        tok_h1s = None
        if own:
            for kc in range(KC):
                tok_h1s = K.e("sync", lambda s, a=h1b[kc], d=P["h1d"][kc, :, sl * TS:(sl + 1) * TS]:
                              s.dma_start(out=d, in_=a),
                              waits=[tok_h1] if kc == 0 else [], inc="SX", inck=16)
        for tc in range(8):
            ktm = K.ab(ABF_KTM + tc * TS, TS)
            vv = K.ab(ABF_VAUG + tc * 1028, 1028)
            for which, (pnm, wofs) in enumerate((("PH0", 0), ("PH1", 8192))):
                unit = ps[pnm]
                mt = None
                for cg in range(2):
                    for kc in range(KC):
                        w0 = []
                        if cg == 0 and kc == 0:
                            w0 = [tok_h1, K.pwait(pnm)]
                            if which == 1:
                                w0.append(("SW", tok_wv))
                            else:
                                w0.append(("SW", tok_wk))
                        mt = K.e("tensor", lambda t, bk=unit[:, cg * 512:(cg + 1) * 512],
                                 l=h1b[kc][:, tc * 128:(tc + 1) * 128],
                                 r=wkv[:, wofs + kc * 1024 + cg * 512:wofs + kc * 1024 + (cg + 1) * 512],
                                 st=(kc == 0), sp=(kc == KC - 1):
                                 t.matmul(bk, l, r, start=st, stop=sp),
                                 waits=w0, inc="SP" if kc == KC - 1 else None)
                if which == 0:
                    tk = K.e("scalar", lambda a, d=ktm, u=unit:
                             a.activation(d, u[:, 0:TS], AF.Exp, scale=SCALE),
                             waits=[("SP", mt)], inc="SA")
                    K.pdone(pnm, ("SA", tk))
                    if sl == 3 and tc == 0 and os.environ.get("MK_DBG", "").find("KP0") >= 0:
                        tkp = K.e("vector", lambda v, d=sb["AFa"][:, AF_TMP:AF_TMP + 512],
                                  u=unit[:, 0:512]: v.tensor_copy(d, u),
                                  waits=[("SA", tk)], inc="SV")
                        K.dbg("KP0", sb["AFa"][:, AF_TMP:AF_TMP + 512], ("SV", tkp))
                        K.pdone(pnm, ("SV", tkp))
                else:
                    tk = None
                    for h in range(NH):
                        w0 = [("SP", mt)]
                        if tc == 0 and h == 0 and sl == 0:
                            w0.append(("SV", tok_ones))
                        tk = K.e("vector", lambda v, d=vv[:, h * 257:h * 257 + 256],
                                 u=unit[:, h * 256:(h + 1) * 256]:
                                 v.tensor_copy(d, u), waits=w0, inc="SV")
                    K.pdone(pnm, ("SV", tk))
        for h in range(NH):
            for ds in range(2):
                pnm = ("PY0", "PY1")[(h * 2 + ds) % 2]
                unit = ps[pnm]
                mt = None
                for tc in range(8):
                    ktm = K.ab(ABF_KTM + tc * TS, TS)
                    vv = K.ab(ABF_VAUG + tc * 1028, 1028)
                    w0 = []
                    if tc == 0:
                        w0 = [K.pwait(pnm), ("SA", K.b.tok("SA")), ("SV", K.b.tok("SV"))]
                    mt = K.e("tensor", lambda t, bk=unit[:, 0:257],
                             l=ktm[:, h * 256 + ds * 128:h * 256 + ds * 128 + 128],
                             r=vv[:, h * 257:h * 257 + 257],
                             st=(tc == 0), sp=(tc == 7):
                             t.matmul(bk, l, r, start=st, stop=sp),
                             waits=w0, inc="SP" if tc == 7 else None)
                tk = K.e("vector", lambda v, d=sb["U_sb"][:, (h * 2 + ds) * 257:(h * 2 + ds + 1) * 257],
                         u=unit[:, 0:257]:
                         v.tensor_tensor(d, u, d, ALU.add),
                         waits=[("SP", mt)], inc="SV")
                K.pdone(pnm, ("SV", tk))
        if sl == 3:
            K.dbg("H1R", h1b[0], tok_h1)
            K.dbg("KT0", K.ab(ABF_KTM + 0 * TS, TS), ("SP", K.b.tok("SP")))
            K.dbg("VA0", K.ab(ABF_VAUG + 0 * 1028, 1028), ("SP", K.b.tok("SP")))
        kv_done_toks.append(("SP", K.b.tok("SP")))
        K._xs_free = [("SX", tok_h1s)] if own else [tok_h1]
        K._castb_free = [("SP", K.b.tok("SP"))] + ([("SX", tok_h1s)] if own else [])
    tok_uw = K.e("sync", lambda s: _exch_write_u(K, s),
                 waits=[("SV", K.b.tok("SV"))], inc="SX", inck=16)
    tok_fw = K.e("sync", lambda s: _exch_write_flag(K, s), waits=[("SX", tok_uw)],
                 inc="SX", inck=16)
    K.e("sync", lambda s: _exch_poll(K, s), waits=[("SX", tok_fw)])
    upart = K.af(AF_XS, 2056)
    tok_ur = K.e("sync", lambda s: _exch_read_u(K, s, upart), inc="SX", inck=16)
    tok_us = K.e("vector", lambda v: v.tensor_tensor(sb["U_sb"][:, :], upart, sb["U_sb"][:, :], ALU.add),
                 waits=[("SX", tok_ur)], inc="SV")
    K._xs_free = [("SV", tok_us)]
    cb = K.ab(ABF_HRAW, 2048)
    tok_c = None
    for hd in range(8):
        K.e("vector", lambda v, hd=hd: v.reciprocal(
            sb["Srec"][:, 0:1], sb["U_sb"][:, hd * 257 + 256:hd * 257 + 257]))
        tok_c = K.e("vector", lambda v, hd=hd: v.tensor_scalar(
            cb[:, hd * 256:(hd + 1) * 256], sb["U_sb"][:, hd * 257:hd * 257 + 256],
            sb["Srec"][:, 0:1], None, ALU.mult), inc="SV")
    K._tok_c = ("SV", tok_c)
    K._cb = cb
    K.dbg("U", sb["U_sb"][:, :], ("SV", tok_us))
    K.dbg("H1", P["h1d"][:, :, :], ("SX", K.b.tok("SX")))
    K.dbg("H0", P["h0d"][:, :, :], ("SX", K.b.tok("SX")))


def _exch_write_u(K, s):
    pid = K.nc.partition_id(engines=[s.engine])
    return s.dma_start(out=K.P["shu"][bass.ts(pid % 2, 1), :, :].rearrange("o p c -> (o p) c"),
                       in_=K.sb["U_sb"][:, :])


def _exch_write_flag(K, s):
    pid = K.nc.partition_id(engines=[s.engine])
    return s.dma_start(out=K.P["shf"][bass.ts(pid % 2, 1), :, :].rearrange("o p c -> (o p) c"),
                       in_=K.sb["nonce_sb"][:, :])


def _exch_read_u(K, s, upart):
    pid = K.nc.partition_id(engines=[s.engine])
    return s.dma_start(out=upart,
                       in_=K.P["shu"][bass.ts((pid + 1) % 2, 1), :, :].rearrange("o p c -> (o p) c"))


def _exch_poll(K, s):
    nc = K.nc
    pid = nc.partition_id(engines=[s.engine])
    psem = K.b.semh["PSEM"]
    r_n0 = s.alloc_register("pn0")
    r_n1 = s.alloc_register("pn1")
    r_f0 = s.alloc_register("pf0")
    r_f1 = s.alloc_register("pf1")
    r_d0 = s.alloc_register("pd0")
    r_d1 = s.alloc_register("pd1")
    r_c = s.alloc_register("pc")
    s.load(r_n0, K.sb["nonce_sb"][0:1, 0:1])
    s.load(r_n1, K.sb["nonce_sb"][0:1, 1:2])
    s.reg_mov(r_c, 0)
    flag_src = K.P["shf"][bass.ts((pid + 1) % 2, 1), :, :].rearrange("o p c -> (o p) c")

    def cond():
        s.dma_start(out=K.sb["flag_sb"][:, :], in_=flag_src).then_inc(psem, 16)
        s.reg_add(r_c, r_c, 16)
        s.wait_ge(psem, r_c)
        s.load(r_f0, K.sb["flag_sb"][0:1, 0:1])
        s.load(r_f1, K.sb["flag_sb"][0:1, 1:2])
        s.reg_alu(r_d0, r_f0, r_n0, ALU.bitwise_xor)
        s.reg_alu(r_d1, r_f1, r_n1, ALU.bitwise_xor)
        s.reg_alu(r_d0, r_d0, r_d1, ALU.bitwise_or)
        return r_d0

    with s.While(cond):
        pass
    return None



P3_H1B = 10240
P3_QB = 26624
P3_RQB = ABF_BM
AF_SQ = 0
AF_BQ = [2048, 2560]
AF_HRR = [3072, 3584]
AF_TA = 4096
AF_XB2 = [4608, 5120]
AF_TB = 8192


def stream_lhsT_groups(K, n, src_of, width, slots, compute):
    sb = K.sb
    loads = {}

    def load(g):
        slot = slots[g % len(slots)]
        dst = sb["WR"][:, slot:slot + width]
        src = src_of(g)
        tok = K.e("gpsimd", lambda gg, d=dst, a=src: gg.dma_start(
            out=d.rearrange("q (a m) -> q a m", a=width // 128), in_=a),
            waits=[K.wslot_free.get(("w1", slot))], inc="SW", inck=16)
        loads[g] = (tok, dst, slot)
        return tok

    for g in range(n):
        load(g)
        if g >= 1:
            tok, dst, slot = loads[g - 1]
            mt = compute(g - 1, dst, tok)
            K.wslot_free[("w1", slot)] = ("SP", mt)
    tok, dst, slot = loads[n - 1]
    mt = compute(n - 1, dst, tok)
    K.wslot_free[("w1", slot)] = ("SP", mt)


def emit_p3(K):
    sb, ps, P = K.sb, K.ps, K.P
    ntt = 4
    h1b = [K.ab(P3_H1B + kc * T_OWN, T_OWN) for kc in range(KC)]
    qb = [K.ab(P3_QB + kc * T_OWN, T_OWN) for kc in range(KC)]
    barrier_pe = ("SP", K.b.tok("SP"))
    tok_h1 = None
    for kc in range(KC):
        w0 = [barrier_pe] if kc == 0 else []
        tok_h1 = K.e("sync", lambda s, d=h1b[kc], a=P["h1d"][kc, :, :]:
                     s.dma_start(out=d, in_=a), waits=w0, inc="SX", inck=16)

    def q_compute(mc, wdst, tok_l):
        mt4 = []
        for tt in range(ntt):
            pnm = ("PH0", "PH1")[tt // 2]
            unit = ps[pnm]
            bank = unit[:, (tt % 2) * 512:(tt % 2) * 512 + 512]
            mt = None
            for kc in range(KC):
                w0 = []
                if kc == 0:
                    w0 = [("SW", tok_l)]
                    if tt == 0:
                        w0 += [("SX", tok_h1), K.pwait("PH0"), K.pwait("PH1")]
                mt = K.e("tensor", lambda t, bk=bank, l=wdst[:, kc * 128:(kc + 1) * 128],
                         r=h1b[kc][:, tt * 512:(tt + 1) * 512],
                         st=(kc == 0), sp=(kc == KC - 1):
                         t.matmul(bk, l, r, start=st, stop=sp),
                         waits=w0, inc="SP" if kc == KC - 1 else None)
            mt4.append(mt)
        t0 = K.e("scalar", lambda a, d=qb[mc][:, 0:1024], u=ps["PH0"]:
                 a.activation(d, u[:, 0:1024], AF.Exp, scale=SCALE),
                 waits=[("SP", mt4[1])], inc="SA")
        K.pdone("PH0", ("SA", t0))
        t1 = K.e("scalar", lambda a, d=qb[mc][:, 1024:2048], u=ps["PH1"]:
                 a.activation(d, u[:, 0:1024], AF.Exp, scale=SCALE),
                 waits=[("SP", mt4[3])], inc="SA")
        K.pdone("PH1", ("SA", t1))
        return mt4[3]

    stream_lhsT_groups(K, 8, lambda g: P["wq"][g], 1024, [0, 1024], q_compute)
    tok_qall = ("SA", K.b.tok("SA"))
    sq = sb["AFa"][0:1, AF_SQ:AF_SQ + 2048]
    rqb = sb["ABFa"][0:1, P3_RQB:P3_RQB + 2048]
    attnb = h1b
    tok_attn = None
    for h in range(NH):
        for tt in range(ntt):
            unit = ps["PY0"]
            bank = unit[0:1, (tt % 2) * 512:(tt % 2) * 512 + 512]
            mt = None
            for ds in range(2):
                w0 = []
                if tt == 0 and ds == 0:
                    w0 = [tok_qall, K.pwait("PY0")]
                if tt >= 2 and ds == 0:
                    w0 = [("SV", K._rt_evac[tt % 2])]
                mt = K.e("tensor", lambda t, bk=bank, l=sb["ones_col"][:, :],
                         r=qb[2 * h + ds][:, tt * 512:(tt + 1) * 512],
                         st=(ds == 0), sp=(ds == 1):
                         t.matmul(bk, l, r, start=st, stop=sp),
                         waits=w0, inc="SP" if ds == 1 else None)
            tk = K.e("vector", lambda v, d=sq[0:1, tt * 512:(tt + 1) * 512], u=bank:
                     v.tensor_copy(d, u), waits=[("SP", mt)], inc="SV")
            K._rt_evac[tt % 2] = tk
        K.pdone("PY0", ("SV", tk))
        K.e("vector", lambda v: v.reciprocal(sq, sq))
        tok_rq = K.e("scalar", lambda a: a.copy(rqb, sq), waits=[("SV", K.b.tok("SV"))],
                     inc="SA")
        for tt in range(ntt):
            unit = ps["PY1"]
            bank = unit[:, (tt % 2) * 512:(tt % 2) * 512 + 512]
            w0 = [("SA", tok_rq)]
            if h == 0 and tt == 0:
                w0.append(K.pwait("PY1"))
            if K._bq_free[tt % 2]:
                w0.append(K._bq_free[tt % 2])
            mtb = K.e("tensor", lambda t, bk=bank,
                      r=rqb[0:1, tt * 512:(tt + 1) * 512]:
                      t.matmul(bk, sb["ones_row"][0:1, 0:128], r, start=True, stop=True),
                      waits=w0, inc="SP")
            bq = sb["AFa"][:, AF_BQ[tt % 2]:AF_BQ[tt % 2] + 512]
            tkb = K.e("vector", lambda v, d=bq, u=bank: v.tensor_copy(d, u),
                      waits=[("SP", mtb)], inc="SV")
            for es in range(2):
                pnm = ("PH0", "PH1")[es]
                unit2 = ps[pnm]
                bank2 = unit2[:, (tt % 2) * 512:(tt % 2) * 512 + 512]
                mt = None
                for ds in range(2):
                    w0 = [K._tok_c, K.pwait(pnm)] if (h == 0 and tt == 0 and ds == 0) else []
                    if ds == 0:
                        w0.append(K._ar_free.get((pnm, tt % 2)))
                    mt = K.e("tensor", lambda t, bk=bank2,
                             l=K._cb[:, (h * 2 + ds) * 256 + es * 128:(h * 2 + ds) * 256 + es * 128 + 128],
                             r=qb[2 * h + ds][:, tt * 512:(tt + 1) * 512],
                             st=(ds == 0), sp=(ds == 1):
                             t.matmul(bk, l, r, start=st, stop=sp),
                             waits=w0, inc="SP" if ds == 1 else None)
                tok_attn = K.e("vector", lambda v, d=attnb[h * 2 + es][:, tt * 512:(tt + 1) * 512],
                               u=bank2, b=bq:
                               v.tensor_tensor(d, u, b, ALU.mult),
                               waits=[("SP", mt), ("SV", tkb)], inc="SV")
                K._ar_free[(pnm, tt % 2)] = ("SV", tok_attn)
            K._bq_free[tt % 2] = ("SV", tok_attn)
    tok_attn_all = ("SV", K.b.tok("SV"))
    K.dbg("ATT", attnb[0], tok_attn_all)
    K.dbg("QB", qb[0], tok_qall)
    K.dbg("RQ", sb["AFa"][0:1, AF_SQ:AF_SQ + 2048], tok_attn_all)
    K.pdone("PH0", tok_attn_all)
    K.pdone("PH1", tok_attn_all)
    K.pdone("PY1", tok_attn_all)

    def wo_compute(mc, wdst, tok_l):
        mt = None
        for tt in range(ntt):
            pnm = ("PY0", "PY1")[mc % 2]
            unit = ps[pnm]
            bank = unit[:, (tt % 2) * 512:(tt % 2) * 512 + 512]
            hr = sb["AFa"][:, AF_HRR[tt % 2]:AF_HRR[tt % 2] + 512]
            tok_hr = K.e("sync", lambda s, d=hr, a=P["h0d"][mc, :, tt * 512:(tt + 1) * 512]:
                         s.dma_start(out=d, in_=a),
                         waits=[K._hr_free.get(tt % 2)], inc="SX", inck=16)
            mtg = None
            for kc in range(KC):
                w0 = []
                if kc == 0:
                    w0 = [("SW", tok_l), tok_attn_all, K.pwait(pnm)]
                mtg = K.e("tensor", lambda t, bk=bank, l=wdst[:, kc * 128:(kc + 1) * 128],
                          r=attnb[kc][:, tt * 512:(tt + 1) * 512],
                          st=(kc == 0), sp=(kc == KC - 1):
                          t.matmul(bk, l, r, start=st, stop=sp),
                          waits=w0, inc="SP" if kc == KC - 1 else None)
            ta = sb["AFa"][:, AF_TA:AF_TA + 512]
            t1 = K.e("scalar", lambda a, d=ta, u=bank, col=mc:
                     a.activation(d, u, AF.Identity, bias=sb["bo_sb"][:, col:col + 1]),
                     waits=[("SP", mtg), K._ta_free], inc="SA")
            K.pdone(pnm, ("SA", t1))
            xb = sb["AFa"][:, AF_XB2[tt % 2]:AF_XB2[tt % 2] + 512]
            t2 = K.e("vector", lambda v, d=xb, a=ta, b=hr:
                     v.tensor_tensor(d, a, b, ALU.add),
                     waits=[("SA", t1), ("SX", tok_hr), K._xb_free.get(tt % 2)], inc="SV")
            tok_x2 = K.e("sync", lambda s, a=xb, d=P["x2d"][mc, :, tt * 512:(tt + 1) * 512]:
                         s.dma_start(out=d, in_=a), waits=[("SV", t2)], inc="SX", inck=16)
            K._ta_free = ("SV", t2)
            K._hr_free[tt % 2] = ("SV", t2)
            K._xb_free[tt % 2] = ("SX", tok_x2)
            mt = mtg
        return mt

    stream_lhsT_groups(K, 8, lambda g: P["wo"][g], 1024, [0, 1024], wo_compute)
    K._x2_tok = ("SX", K.b.tok("SX"))
    K.dbg("X2", P["x2d"][:, :, :], K._x2_tok)



def emit_p4(K):
    sb, ps, P = K.sb, K.ps, K.P
    for sl in range(2):
        xs = [K.af(AF_XS + kc * TS, TS) for kc in range(KC)]
        tok_x = None
        for kc in range(KC):
            w0 = []
            if kc == 0:
                w0 = list(K._xs_free or []) + [K._x2_tok]
            tok_x = K.e("sync", lambda s, d=xs[kc], a=P["x2d"][kc, :, sl * TS:(sl + 1) * TS]:
                        s.dma_start(out=d, in_=a), waits=w0, inc="SX", inck=16)
        h2b = [K.ab(ABF_XLNB + kc * TS, TS) for kc in range(KC)]
        tok_h2 = emit_ln(K, xs, TS, h2b, gb=(sb["g2_sb"], sb["c2_sb"]), xin_tok=("SX", tok_x))
        gb = [K.ab(ABF_GB + mc * TS, TS) for mc in range(32)]

        def ff1_compute(mc, wdst, tok_l):
            pnm = ("PH0", "PH1")[mc % 2]
            unit = ps[pnm]
            mt = None
            for tts in range(2):
                bank = unit[:, tts * 512:(tts + 1) * 512]
                for kc in range(KC):
                    w0 = []
                    if kc == 0:
                        w0 = [("SW", tok_l)]
                        if tts == 0:
                            w0 += [tok_h2, K.pwait(pnm)]
                    mt = K.e("tensor", lambda t, bk=bank, l=wdst[:, kc * 128:(kc + 1) * 128],
                             r=h2b[kc][:, tts * 512:(tts + 1) * 512],
                             st=(kc == 0), sp=(kc == KC - 1):
                             t.matmul(bk, l, r, start=st, stop=sp),
                             waits=w0, inc="SP" if kc == KC - 1 else None)
            tg = K.e("scalar", lambda a, d=gb[mc], u=unit, col=mc:
                     a.activation(d, u[:, 0:TS], AF.Gelu_apprx_tanh,
                                  bias=sb["bff1_sb"][:, col:col + 1]),
                     waits=[("SP", mt)], inc="SA")
            K.pdone(pnm, ("SA", tg))
            return mt

        stream_lhsT_groups(K, 32, lambda g: P["ff1"][g], 1024, [0, 1024], ff1_compute)
        tok_gall = ("SA", K.b.tok("SA"))

        loads = {}

        def load_ff2(mc2, half):
            slot = WR_W2[(mc2 % 2) * 2 + half]
            dst = sb["WR"][:, slot:slot + 2048]
            src = P["ff2"][mc2][:, half * 16:(half + 1) * 16, :]
            tok = K.e("gpsimd", lambda g, d=dst, a=src: g.dma_start(
                out=d.rearrange("q (a m) -> q a m", a=16), in_=a),
                waits=[K.wslot_free.get(("w2", slot))], inc="SW", inck=16)
            loads[(mc2, half)] = (tok, dst, slot)

        def ff2_compute(mc2):
            pnm = ("PY0", "PY1")[mc2 % 2]
            unit = ps[pnm]
            mt = None
            for tts in range(2):
                bank = unit[:, tts * 512:(tts + 1) * 512]
                first = True
                for half in range(2):
                    tok_l, wdst, slot = loads[(mc2, half)]
                    for hc in range(16):
                        w0 = []
                        if first:
                            w0 = [("SW", tok_l), tok_gall]
                            if tts == 0:
                                w0.append(K.pwait(pnm))
                        mt = K.e("tensor", lambda t, bk=bank, l=wdst[:, hc * 128:(hc + 1) * 128],
                                 r=gb[half * 16 + hc][:, tts * 512:(tts + 1) * 512],
                                 st=first, sp=(half == 1 and hc == 15):
                                 t.matmul(bk, l, r, start=st, stop=sp),
                                 waits=w0, inc="SP" if (half == 1 and hc == 15) else None)
                        first = False
            tb = sb["AFa"][:, AF_TB:AF_TB + 1024]
            t1 = K.e("scalar", lambda a, d=tb, u=unit, col=mc2:
                     a.activation(d, u[:, 0:TS], AF.Identity,
                                  bias=sb["bff2h_sb"][:, col:col + 1]),
                     waits=[("SP", mt), K._ta_free], inc="SA")
            K.pdone(pnm, ("SA", t1))
            t2 = K.e("vector", lambda v, d=xs[mc2], a=tb:
                     v.tensor_tensor(d[:, 0:TS], a, d[:, 0:TS], ALU.add),
                     waits=[("SA", t1)], inc="SV")
            K._ta_free = ("SV", t2)
            for slot in (loads[(mc2, 0)][2], loads[(mc2, 1)][2]):
                K.wslot_free[("w2", slot)] = ("SP", mt)
            return ("SV", t2)

        last_ev = None
        for mc2 in range(8):
            load_ff2(mc2, 0)
            load_ff2(mc2, 1)
            if mc2 >= 1:
                last_ev = ff2_compute(mc2 - 1)
        last_ev = ff2_compute(7)
        if sl == 0:
            K.dbg("H2", h2b[0], tok_h2)
            K.dbg("GB0", gb[0], tok_gall)
            K.dbg("GB31", gb[31], tok_gall)
        tok_sp = None
        for kc in range(KC):
            tok_sp = K.e("sync", lambda s, a=xs[kc], d=P["x3d"][kc, :, sl * TS:(sl + 1) * TS]:
                         s.dma_start(out=d, in_=a),
                         waits=[("SV", K.b.tok("SV"))] if kc == 0 else [], inc="SX", inck=16)
        K._xs_free = [("SX", tok_sp)]
        K._castb_free = [("SP", K.b.tok("SP")), ("SX", tok_sp)]
    K._x3_tok = ("SX", K.b.tok("SX"))
    K.dbg("X3", K.P["x3d"][:, :, :], K._x3_tok)



def emit_p5(K):
    sb, ps, P = K.sb, K.ps, K.P
    ntt = 4
    xb3 = [K.ab(ABF_XB + kc * T_OWN, T_OWN) for kc in range(KC)]
    barrier_pe = ("SP", K.b.tok("SP"))
    tok_x3 = None
    for kc in range(KC):
        w0 = [K._x3_tok, barrier_pe] if kc == 0 else []
        tok_x3 = K.e("gpsimd", lambda g, d=xb3[kc], a=P["x3d"][kc, :, :]:
                     g.dma_start(out=d, in_=a), waits=w0, inc="SW", inck=16)

    def wd_compute(mc, wdst, tok_l):
        mt = None
        for tt in range(ntt):
            pnm = ("PY0", "PY1")[mc % 2]
            unit = ps[pnm]
            bank = unit[:, (tt % 2) * 512:(tt % 2) * 512 + 512]
            hr = sb["AFa"][:, AF_HRR[tt % 2]:AF_HRR[tt % 2] + 512]
            tok_hr = K.e("sync", lambda s, d=hr, a=P["R1d"][mc, :, tt * 512:(tt + 1) * 512]:
                         s.dma_start(out=d, in_=a),
                         waits=[K._hr_free.get(tt % 2)], inc="SX", inck=16)
            mtg = None
            for kc in range(KC):
                w0 = []
                if kc == 0:
                    w0 = [("SW", max(tok_l, tok_x3)), K.pwait(pnm)]
                mtg = K.e("tensor", lambda t, bk=bank, l=wdst[:, kc * 128:(kc + 1) * 128],
                          r=xb3[kc][:, tt * 512:(tt + 1) * 512],
                          st=(kc == 0), sp=(kc == KC - 1):
                          t.matmul(bk, l, r, start=st, stop=sp),
                          waits=w0, inc="SP" if kc == KC - 1 else None)
            ta = sb["AFa"][:, AF_TA:AF_TA + 512]
            t1 = K.e("scalar", lambda a, d=ta, u=bank, col=mc:
                     a.activation(d, u, AF.Identity, bias=sb["bd_sb"][:, col:col + 1]),
                     waits=[("SP", mtg), K._ta_free], inc="SA")
            K.pdone(pnm, ("SA", t1))
            xb = sb["AFa"][:, AF_XB2[tt % 2]:AF_XB2[tt % 2] + 512]
            t2 = K.e("vector", lambda v, d=xb, a=ta, b=hr:
                     v.tensor_tensor(d, a, b, ALU.add),
                     waits=[("SA", t1), ("SX", tok_hr), K._xb_free.get(tt % 2)], inc="SV")
            tok_r3 = K.e("sync", lambda s, a=xb, d=P["R3d"][mc, :, tt * 512:(tt + 1) * 512]:
                         s.dma_start(out=d, in_=a), waits=[("SV", t2)], inc="SX", inck=16)
            K._ta_free = ("SV", t2)
            K._hr_free[tt % 2] = ("SV", t2)
            K._xb_free[tt % 2] = ("SX", tok_r3)
            mt = mtg
        return mt

    stream_lhsT_groups(K, 8, lambda g: P["wd"][g], 1024, [0, 1024], wd_compute)
    K._r3_tok = ("SX", K.b.tok("SX"))
    K._xs_free = [K._r3_tok]
    K.dbg("R3", P["R3d"][:, :, :], K._r3_tok)
